# version 04
# baseline (speedup 1.0000x reference)
import numpy as np
import ml_dtypes
import concourse.bass as bass
import concourse.mybir as mybir
from concourse.bass_utils import run_bass_kernel_spmd
from contextlib import ExitStack

F32 = mybir.dt.float32
BF16 = mybir.dt.bfloat16
AF = mybir.ActivationFunctionType
ALU = mybir.AluOpType
AX = mybir.AxisListType
BF = ml_dtypes.bfloat16

D = 4096
NK = D // 128
NH, NKV, HD = 32, 8, 128
GRP = NH // NKV
EPS = 1e-5
GATE_NORM = 16.0
INV_SQRT_D = 1.0 / np.sqrt(128.0)

N_CORES = 8
OWN_TOK = 2048
N_OWN_CH = OWN_TOK // 128
N_PRE_CH = 48
TOK_ALL = (N_PRE_CH + N_OWN_CH) * 128
CB = 16

SAFE_CHAIN = {"tensor"}



ENGINES = ("sync", "gpsimd", "tensor", "vector", "scalar")


class Trk:
    __slots__ = ("ready", "users")

    def __init__(self):
        self.ready = {}
        self.users = {}


class Prog:
    def __init__(self):
        self.streams = {e: [] for e in ENGINES}
        self.ctr = {}
        self.water = {e: {} for e in ENGINES}

    def _bump(self, sem, k):
        self.ctr[sem] = self.ctr.get(sem, 0) + k
        return self.ctr[sem]

    def _wait(self, eng, sem, val):
        if val <= 0:
            return
        if self.water[eng].get(sem, 0) >= val:
            return
        self.water[eng][sem] = val
        self.streams[eng].append(
            lambda E, _s=sem, _v=val: E.wait_ge(E._sems[_s], _v))

    def barrier_from(self, eng, sems):
        for s in sems:
            self._wait(eng, s, self.ctr.get(s, 0))

    def full_barrier(self):
        for e in ENGINES:
            for s in list(self.ctr):
                self._wait(e, s, self.ctr[s])

    def step(self, eng, fns, sem=None, inc=1, reads=(), writes=(), rw=(),
             dma=False):
        if not isinstance(fns, (list, tuple)):
            fns = [fns]
        reads = list(reads) + list(rw)
        writes = list(writes) + list(rw)
        for r in reads:
            for s, (v, peng) in r.ready.items():
                if peng == eng and eng in SAFE_CHAIN and not dma:
                    continue
                self._wait(eng, s, v)
        for w in writes:
            for s, v in w.users.items():
                self._wait(eng, s, v)
            for s, (v, peng) in w.ready.items():
                if peng == eng and eng in SAFE_CHAIN and not dma:
                    continue
                self._wait(eng, s, v)
        for fn in fns[:-1]:
            self.streams[eng].append(fn)
        if sem is None:
            self.streams[eng].append(fns[-1])
            val = None
        else:
            val = self._bump(sem, inc)
            self.streams[eng].append(
                lambda E, _f=fns[-1], _s=sem, _k=inc: _f(E).then_inc(E._sems[_s], _k))
            for r in reads:
                r.users[sem] = val
            for w in writes:
                w.ready[sem] = (val, None if dma else eng)
                w.users = {}
        return val

    def dma(self, eng, out, in_, sem, reads=(), writes=(), **kw):
        return self.step(
            eng, lambda E: E.dma_start(out=out, in_=in_, **kw),
            sem=sem, inc=16, reads=reads, writes=writes, dma=True)


class Arena:

    def __init__(self, nc, name, nbytes):
        self.t = None
        self.nc = nc
        self.name = name
        self.nbytes = nbytes
        self.off = 0
        self.marks = []

    def open(self, ctx):
        self.t = ctx.enter_context(
            self.nc.sbuf_tensor(self.name, [128, self.nbytes], mybir.dt.uint8))
        return self

    def alloc(self, cols, dtype, ncopies=1):
        es = mybir.dt.size(dtype)
        out = []
        for _ in range(ncopies):
            off = (self.off + 63) & ~63
            self.off = off + cols * es
            assert self.off <= self.nbytes, (
                f"arena {self.name} overflow: {self.off} > {self.nbytes}")
            ap = self.t.ap()[:, off:off + cols * es].bitcast(dtype)
            out.append(ap)
        return out[0] if ncopies == 1 else out

    def push(self):
        self.marks.append(self.off)

    def pop(self):
        self.off = self.marks.pop()



def build_program(n_pre_ch=N_PRE_CH, n_own_ch=N_OWN_CH, phases=("p0",),
                  debug_taps=(), p1_level=99, p1_groups=NKV, o_parts="both"):
    n_ch = n_pre_ch + n_own_ch
    cb = min(CB, n_ch)
    assert n_ch % cb == 0
    n_batch = n_ch // cb

    nc = bass.Bass("TRN2", target_bir_lowering=False, debug=False)

    xT_in = nc.dram_tensor("xT_in", [n_ch, NK, 128, 128], F32,
                           kind="ExternalInput").ap()
    consts = nc.dram_tensor("consts", [8, 128, 128], F32,
                            kind="ExternalInput").ap()
    n_sw = n_own_ch + 1
    wqkv_in = ropeq_in = ropek_in = maskR_in = maskA_in = trimask_in = None
    if "ppre" in phases or "p1" in phases:
        wqkv_in = nc.dram_tensor("wqkv_in", [NKV, NK, 128, 768], F32,
                                 kind="ExternalInput").ap()
    if "p1" in phases:
        ropeq_in = nc.dram_tensor("ropeq_in", [n_sw, 2, 128, 128], F32,
                                  kind="ExternalInput").ap()
        ropek_in = nc.dram_tensor("ropek_in", [n_sw, 2, 128, 128], F32,
                                  kind="ExternalInput").ap()
        maskR_in = nc.dram_tensor("maskR_in", [128, 768], F32,
                                  kind="ExternalInput").ap()
        maskA_in = nc.dram_tensor("maskA_in", [128, 768], F32,
                                  kind="ExternalInput").ap()
        trimask_in = nc.dram_tensor("trimask_in", [128, 512], F32,
                                    kind="ExternalInput").ap()

    wo_in = wbT_in = wr_in = at_in = bf_in = b_in = None
    if "p2" in phases:
        wo_in = nc.dram_tensor("wo_in", [NK, NK, 128, 128], F32,
                               kind="ExternalInput").ap()
    if "p4" in phases:
        wbT_in = nc.dram_tensor("wbT_in", [NK, NK, 128, 128], F32,
                                kind="ExternalInput").ap()
        wr_in = nc.dram_tensor("wr_in", [NK, 128, 8], F32,
                               kind="ExternalInput").ap()
        at_in = nc.dram_tensor("at_in", [NK, 128, 128], F32,
                               kind="ExternalInput").ap()
        bf_in = nc.dram_tensor("bf_in", [NK, 128, 128], F32,
                               kind="ExternalInput").ap()
        b_in = nc.dram_tensor("b_in", [128, NK], F32,
                              kind="ExternalInput").ap()

    ln1T = nc.dram_tensor(
        "ln1T", [n_ch, NK, 128, 128], BF16,
        kind="ExternalOutput" if "ln1T" in debug_taps else "Internal").ap()
    oT_dram = None
    if "p1" in phases:
        oT_dram = nc.dram_tensor(
            "oT_dram", [n_own_ch, NKV, 128, 512], BF16,
            kind="ExternalOutput" if "oT" in debug_taps else "Internal").ap()
    dbg_bf = dbg_f32 = None
    if "gdbg" in debug_taps:
        dbg_bf = nc.dram_tensor("dbg_bf", [4, 128, 512], BF16,
                                kind="ExternalOutput").ap()
        dbg_f32 = nc.dram_tensor("dbg_f32", [2, 128, 512], F32,
                                 kind="ExternalOutput").ap()
    x1T_dram = None
    if "p2" in phases:
        x1T_dram = nc.dram_tensor(
            "x1T_dram", [NK, 128, n_own_ch * 128], BF16,
            kind="ExternalOutput" if "x1T" in debug_taps else "Internal").ap()
    out_hbm = None
    if "p4" in phases:
        out_hbm = nc.dram_tensor("out_hbm", [n_own_ch * 128, D], F32,
                                 kind="ExternalOutput").ap()
    ln2dbg = None
    if "ln2T" in debug_taps:
        ln2dbg = nc.dram_tensor("ln2dbg", [NK, 128, n_own_ch * 128], BF16,
                                kind="ExternalOutput").ap()
    sin_out = None
    if "Sin" in debug_taps:
        sin_out = nc.dram_tensor("sin_out", [NKV, 128, 128], F32,
                                 kind="ExternalOutput").ap()

    p = Prog()

    with ExitStack() as ctx:
        E = ctx.enter_context
        AR = Arena(nc, "arena", 205 * 1024).open(ctx)

        ident_f = AR.alloc(128, F32)
        tri16 = AR.alloc(128, F32)
        onehot = AR.alloc(1, F32)
        ones_col_f = AR.alloc(1, F32)
        ones_row_f = AR.alloc(128, F32)
        ones_col_b = AR.alloc(1, BF16)
        ident_b = AR.alloc(128, BF16)
        eps_col = AR.alloc(1, F32)

        t_cst = Trk()
        p.dma("sync", ident_f, consts[0], "CST", writes=[t_cst])
        p.dma("sync", tri16, consts[1], "CST", writes=[t_cst])
        p.dma("sync", onehot, consts[2, :, 0:1], "CST", writes=[t_cst],
              allow_slow_non_contiguous=True)
        p.dma("sync", ones_col_f, consts[3, :, 0:1], "CST", writes=[t_cst],
              allow_slow_non_contiguous=True)
        p.dma("sync", ones_row_f[0:1, :], consts[3, 0:1, :], "CST", writes=[t_cst])
        p.dma("gpsimd", ones_col_b, consts[3, :, 0:1], "CSG", writes=[t_cst],
              allow_slow_non_contiguous=True)
        p.dma("sync", eps_col, consts[4, :, 0:1], "CST", writes=[t_cst],
              allow_slow_non_contiguous=True)
        p.dma("gpsimd", ident_b, consts[0], "CSG", writes=[t_cst])

        AR.push()
        xbf = [AR.alloc(512, BF16) for _ in range(2)]
        x2b = [AR.alloc(512, BF16) for _ in range(2)]
        xf32 = [AR.alloc(cb * 128, F32) for _ in range(2)]
        lnout = [AR.alloc(cb * 128, BF16) for _ in range(2)]
        srow = AR.alloc(2 * cb * 128, F32)
        mrow = AR.alloc(cb * 128, F32)
        rrow = AR.alloc(cb * 128, F32)
        rep_m = AR.alloc(cb * 128, F32)
        rep_r = AR.alloc(cb * 128, F32)

        ps = [E(nc.psum_tensor(f"ps{i}", [128, 512], F32)) for i in range(8)]
        ps_sx, ps_sx2 = ps[0][0:1, :], ps[1][0:1, :]

        t_xbf = [Trk(), Trk()]
        t_x2b = [Trk(), Trk()]
        t_xf = [Trk(), Trk()]
        t_ln = [Trk(), Trk()]
        t_srow, t_mrow, t_rrow = Trk(), Trk(), Trk()
        t_rep = Trk()
        t_pssum = Trk()
        t_psrep = Trk()

        ntb = cb * 128 // 512
        for b in range(n_batch):
            ch0 = b * cb
            for tb in range(ntb):
                cA = ch0 + tb * 4
                for k in range(NK):
                    bi = k % 2
                    p.dma("gpsimd", xbf[bi].rearrange("p (c t) -> p c t", c=4),
                          xT_in[cA:cA + 4, k].rearrange("c p t -> p c t"),
                          f"XB{bi}", writes=[t_xbf[bi]])
                    p.step("scalar", lambda E, _i=bi: E.activation(
                        x2b[_i], xbf[_i], AF.Square), sem="AC",
                        reads=[t_xbf[bi]], writes=[t_x2b[bi]])
                    st_, sp_ = (k == 0), (k == NK - 1)
                    p.step("tensor", [
                        lambda E, _i=bi, _s=st_, _p=sp_: E.matmul(
                            ps_sx, ones_col_b, xbf[_i], start=_s, stop=_p),
                        lambda E, _i=bi, _s=st_, _p=sp_: E.matmul(
                            ps_sx2, ones_col_b, x2b[_i], start=_s, stop=_p),
                    ], sem="PE", reads=[t_xbf[bi], t_x2b[bi], t_cst],
                        writes=[t_pssum])
                p.step("scalar", [
                    lambda E, _t=tb: E.copy(srow[0:1, _t * 512:(_t + 1) * 512], ps_sx),
                    lambda E, _t=tb: E.copy(
                        srow[0:1, cb * 128 + _t * 512:cb * 128 + (_t + 1) * 512],
                        ps_sx2),
                ], sem="AC", rw=[t_pssum], writes=[t_srow])
            NT = cb * 128
            p.step("vector", lambda E: E.tensor_scalar(
                mrow[0:1, :], srow[0:1, 0:NT], 1.0 / D, None, op0=ALU.mult),
                sem="DV", reads=[t_srow], writes=[t_mrow])
            p.step("vector", lambda E: E.tensor_scalar(
                rrow[0:1, :], srow[0:1, NT:2 * NT], 1.0 / D, None, op0=ALU.mult),
                sem="DV", reads=[t_srow], writes=[t_rrow])
            p.step("scalar", lambda E: E.activation(
                srow[0:1, 0:NT], mrow[0:1, :], AF.Square), sem="AC",
                reads=[t_mrow], writes=[t_srow])
            p.step("vector", lambda E: E.tensor_tensor(
                rrow[0:1, :], rrow[0:1, :], srow[0:1, 0:NT], op=ALU.subtract),
                sem="DV", reads=[t_srow], rw=[t_rrow])
            p.step("scalar", lambda E: E.activation(
                rrow[0:1, :], rrow[0:1, :], AF.Ln, bias=eps_col[0:1, :]),
                sem="AC", rw=[t_rrow])
            p.step("scalar", lambda E: E.activation(
                rrow[0:1, :], rrow[0:1, :], AF.Exp, scale=-0.5),
                sem="AC", rw=[t_rrow])
            for piece in range(ntb):
                sl = slice(piece * 512, (piece + 1) * 512)
                p.step("tensor", [
                    lambda E, _s=sl: E.matmul(ps[2][:], ones_row_f[0:1, :],
                                              mrow[0:1, _s], start=True, stop=True),
                    lambda E, _s=sl: E.matmul(ps[3][:], ones_row_f[0:1, :],
                                              rrow[0:1, _s], start=True, stop=True),
                ], sem="PE", reads=[t_mrow, t_rrow, t_cst], writes=[t_psrep])
                p.step("scalar", [
                    lambda E, _s=sl: E.copy(rep_m[:, _s], ps[2][:]),
                    lambda E, _s=sl: E.copy(rep_r[:, _s], ps[3][:]),
                ], sem="AC", rw=[t_psrep], writes=[t_rep])
            for k in range(NK):
                bi = k % 2
                p.dma("sync", xf32[bi].rearrange("p (c t) -> p c t", c=cb),
                      xT_in[ch0:ch0 + cb, k].rearrange("c p t -> p c t"),
                      f"XF{bi}", writes=[t_xf[bi]])
                p.step("vector", lambda E, _i=bi: E.tensor_tensor(
                    xf32[_i], xf32[_i], rep_m, op=ALU.subtract),
                    sem="DV", reads=[t_rep], rw=[t_xf[bi]])
                p.step("vector", lambda E, _i=bi: E.tensor_tensor(
                    lnout[_i], xf32[_i], rep_r, op=ALU.mult),
                    sem="DV", reads=[t_rep, t_xf[bi]], writes=[t_ln[bi]])
                p.dma("sync",
                      ln1T[ch0:ch0 + cb, k].rearrange("c p t -> p c t"),
                      lnout[bi].rearrange("p (c t) -> p c t", c=cb),
                      f"LS{bi}", reads=[t_ln[bi]])
        AR.pop()

        if "ppre" in phases or "p1" in phases:
            p.full_barrier()
            AR.push()

            from types import SimpleNamespace

            S = [[AR.alloc(128, F32) for _ in range(2)] for _ in range(NKV)]
            S_bf = [[AR.alloc(128, BF16) for _ in range(2)] for _ in range(NKV)]
            t_S = [[Trk(), Trk()] for _ in range(NKV)]
            t_Sbf = [[Trk(), Trk()] for _ in range(NKV)]
            Vring = [AR.alloc(128, BF16) for _ in range(2)]
            t_V = [Trk(), Trk()]

            def mk_kv_ns():
                n = SimpleNamespace()
                for nm, cols, dt in [("ek", 128, F32), ("absk", 128, F32),
                                     ("gtmp", 128, F32), ("kmin", 128, F32),
                                     ("gs", 128, F32), ("em", 128, F32),
                                     ("ep", 128, F32), ("ktf", 128, F32),
                                     ("ktil", 128, BF16), ("sigk", 1, F32),
                                     ("recipk", 1, F32), ("gam", 1, F32),
                                     ("stmp", 128, F32), ("krb", 128, BF16),
                                     ("t1k", 128, F32), ("t2k", 128, F32),
                                     ("ktT", 128, BF16)]:
                    setattr(n, nm, AR.alloc(cols, dt))
                    setattr(n, "t_" + nm, Trk())
                return n

            KV = [mk_kv_ns(), mk_kv_ns()]
            FQ = [ps[0][:, :], ps[6][:, :]]
            t_FQ = [Trk(), Trk()]
            FKV = [ps[1][:, 0:256], ps[7][:, 0:256]]
            FG = [ps[1][:, 256:384], ps[7][:, 256:384]]
            FSp = [ps[1][:, 384:512], ps[7][:, 384:512]]
            t_FK = [Trk(), Trk()]
            ps_gam = ps[2][:, 508:509]
            t_ps2 = Trk()
            t_ps5 = Trk()
            t_sc34 = [Trk(), Trk()]
            tq_ctr = [0]

            def emit_kv_math(B, pr, do_gla, v_slot):
                if do_gla:
                    p.step("scalar", lambda E, _B=B, _P=FKV[pr]: E.activation(
                        _B.ek, _P[:, 0:128], AF.Exp, accum_out=_B.sigk),
                        sem="AC", rw=[t_FK[pr]], writes=[B.t_ek, B.t_sigk])
                    p.step("vector", lambda E, _B=B: E.reciprocal(
                        _B.recipk, _B.sigk), sem="DV",
                        reads=[B.t_sigk], writes=[B.t_recipk])
                    p.step("scalar", lambda E, _B=B, _P=FKV[pr]: E.activation(
                        _B.absk, _P[:, 0:128], AF.Abs), sem="AC",
                        rw=[t_FK[pr]], writes=[B.t_absk])
                    p.step("scalar", lambda E, _B=B: E.activation(
                        _B.gtmp, _B.absk, AF.Exp, scale=-1.0), sem="AC",
                        reads=[B.t_absk], writes=[B.t_gtmp])
                    p.step("scalar", lambda E, _B=B: E.activation(
                        _B.gtmp, _B.gtmp, AF.Ln, bias=ones_col_f), sem="AC",
                        rw=[B.t_gtmp])
                    p.step("vector", lambda E, _B=B, _P=FKV[pr]: E.tensor_scalar(
                        _B.kmin, _P[:, 0:128], 0.0, None, op0=ALU.min),
                        sem="DV", rw=[t_FK[pr]], writes=[B.t_kmin])
                    p.step("vector", lambda E, _B=B: E.tensor_tensor(
                        _B.gs, _B.kmin, _B.gtmp, op=ALU.subtract),
                        sem="DV", reads=[B.t_kmin, B.t_gtmp], writes=[B.t_gs])
                    p.step("tensor", lambda E, _B=B, _G=FG[pr]: E.matmul(
                        _G, tri16, _B.gs, start=True, stop=True),
                        sem="PE", reads=[B.t_gs, t_cst], rw=[t_FK[pr]])
                    p.step("scalar", [
                        lambda E, _B=B, _G=FG[pr]: E.activation(
                            _B.em, _G, AF.Exp, scale=-1.0),
                        lambda E, _B=B, _G=FG[pr]: E.activation(
                            _B.ep, _G, AF.Exp),
                    ], sem="AC", rw=[t_FK[pr]], writes=[B.t_em, B.t_ep])
                    p.step("vector", lambda E, _B=B: E.tensor_tensor(
                        _B.ktf, _B.ek, _B.em, op=ALU.mult),
                        sem="DV", reads=[B.t_ek, B.t_em], writes=[B.t_ktf])
                    p.step("vector", lambda E, _B=B: E.tensor_scalar(
                        _B.ktil, _B.ktf, _B.recipk, None, op0=ALU.mult),
                        sem="DV", reads=[B.t_ktf, B.t_recipk],
                        writes=[B.t_ktil])
                    p.step("tensor", lambda E, _B=B: E.matmul(
                        ps_gam, _B.ep, onehot, start=True, stop=True),
                        sem="PE", reads=[B.t_ep, t_cst], rw=[t_ps2])
                    p.step("scalar", lambda E, _B=B: E.copy(_B.gam, ps_gam),
                           sem="AC", rw=[t_ps2], writes=[B.t_gam])
                p.step("scalar", lambda E, _s=v_slot, _P=FKV[pr]: E.copy(
                    Vring[_s], _P[:, 128:256]),
                    sem="AC", rw=[t_FK[pr]], writes=[t_V[v_slot]])

            def emit_state_update(B, pr, g, par, v_slot):
                p.step("tensor", lambda E, _B=B, _s=v_slot, _Sp=FSp[pr]:
                       E.matmul(_Sp, _B.ktil, Vring[_s], start=True,
                                stop=True),
                       sem="PE", reads=[B.t_ktil, t_V[v_slot]], rw=[t_FK[pr]])
                p.step("vector", lambda E, _B=B, _g=g, _p=par, _Sp=FSp[pr]:
                       E.tensor_tensor(_B.stmp, S[_g][1 - _p], _Sp,
                                       op=ALU.add),
                       sem="DV", reads=[t_S[g][1 - par]], rw=[t_FK[pr]],
                       writes=[B.t_stmp])
                p.step("vector", lambda E, _B=B, _g=g, _p=par: E.tensor_scalar(
                    S[_g][_p], _B.stmp, _B.gam, None, op0=ALU.mult),
                    sem="DV", reads=[B.t_stmp, B.t_gam], writes=[t_S[g][par]])

            def transpose_to(dst_ap, src_ap, t_src, t_dst, sem, qn="sync"):
                qn = "sync"
                p.step(qn, lambda E, _d=dst_ap, _s=src_ap: E.dma_start(
                    out=_d, in_=_s, transpose=True), sem=sem, inc=16,
                    reads=[t_src], writes=[t_dst], dma=True)

        if "ppre" in phases:
            AR.push()
            wkv = AR.alloc(NKV * NK * 256, BF16)
            wkv_v = wkv.rearrange("p (g k c) -> p g k c", g=NKV, k=NK)
            t_wkv = Trk()
            for g in range(NKV):
                p.dma("gpsimd", wkv_v[:, g],
                      wqkv_in[g, :, :, 512:768].rearrange("k p c -> p k c"),
                      "WKV", writes=[t_wkv])
            lnst = [AR.alloc(NK * 128, BF16) for _ in range(2)]
            t_lnst = [Trk(), Trk()]
            for g in range(NKV):
                p.step("vector", lambda E, _g=g: E.memset(S[_g][0], 0.0),
                       sem="DV", writes=[t_S[g][0]])
            it = 0
            for c in range(n_pre_ch):
                sl = c % 2
                p.dma("sync", lnst[sl].rearrange("p (k t) -> p k t", k=NK),
                      ln1T[c].rearrange("k p t -> p k t"), f"LN{sl}",
                      writes=[t_lnst[sl]])
                for g in range(NKV):
                    pr = it % 2
                    it += 1
                    par = (c + 1) % 2
                    B = KV[pr]
                    fns = []
                    for k in range(NK):
                        fns.append(lambda E, _g=g, _k=k, _s=sl, _P=FKV[pr]:
                                   E.matmul(
                            _P, lnst[_s].rearrange(
                                "p (k t) -> p k t", k=NK)[:, _k, :],
                            wkv_v[:, _g, _k, :],
                            start=(_k == 0), stop=(_k == NK - 1)))
                    p.step("tensor", fns, sem="PE",
                           reads=[t_lnst[sl], t_wkv], rw=[t_FK[pr]])
                    emit_kv_math(B, pr, do_gla=True, v_slot=0)
                    emit_state_update(B, pr, g, par, 0)
            if "Sin" in debug_taps:
                par = n_pre_ch % 2
                sdbg = AR.alloc(128, F32)
                t_sdbg = Trk()
                for g in range(NKV):
                    p.step("scalar", lambda E, _g=g, _p=par: E.copy(
                        sdbg, S[_g][_p]), sem="AC",
                        reads=[t_S[g][par]], writes=[t_sdbg])
                    p.dma("sync", sin_out[g], sdbg, "SO", reads=[t_sdbg])
            AR.pop()

        if "p1" in phases:
            p.full_barrier()
            AR.push()
            wqb = [AR.alloc(NK * 768, BF16) for _ in range(2)]
            t_wq = [Trk(), Trk()]
            tblq = AR.alloc(n_sw * 2 * 128, BF16)
            tblq_v = tblq.rearrange("p (s i c) -> p s i c", s=n_sw, i=2)
            tblk = AR.alloc(n_sw * 2 * 128, BF16)
            tblk_v = tblk.rearrange("p (s i c) -> p s i c", s=n_sw, i=2)
            maskR = AR.alloc(768, BF16)
            maskA = AR.alloc(768, BF16)
            trim = AR.alloc(512, BF16)
            t_tbl = Trk()
            p.dma("gpsimd", tblq_v, ropeq_in.rearrange("s i p c -> p s i c"),
                  "TBL", writes=[t_tbl])
            p.dma("gpsimd", tblk_v, ropek_in.rearrange("s i p c -> p s i c"),
                  "TB2", writes=[t_tbl])
            p.dma("gpsimd", maskR, maskR_in, "TB3", writes=[t_tbl])
            p.dma("gpsimd", maskA, maskA_in, "TB4", writes=[t_tbl])
            p.dma("gpsimd", trim, trimask_in, "TB5", writes=[t_tbl])

            lnp = [AR.alloc(NK * 128, BF16) for _ in range(2)]
            t_lnp = [Trk(), Trk()]
            krT = [AR.alloc(128, BF16) for _ in range(2)]
            t_krT = [Trk(), Trk()]

            def mk_q_ns():
                n = SimpleNamespace()
                for nm, cols, dt in [("eq", 512, F32), ("sigq", 4, F32),
                                     ("recipq", 4, F32), ("qhu", 512, F32),
                                     ("qhat", 512, BF16), ("qt1", 512, F32),
                                     ("qt2", 512, F32), ("qrb", 512, BF16),
                                     ("qhT", 512, BF16), ("qrT", 512, BF16)]:
                    setattr(n, nm, AR.alloc(cols, dt))
                    setattr(n, "t_" + nm, Trk())
                return n

            QN = [mk_q_ns(), mk_q_ns()]
            atm = AR.alloc(512, BF16)
            pu = AR.alloc(768, F32)
            pm = AR.alloc(768, F32)
            ssum4 = AR.alloc(4, F32)
            rswa = AR.alloc(4, F32)
            pb = AR.alloc(4 * 256, BF16)
            ptA = AR.alloc(512, BF16)
            ptB = AR.alloc(512, BF16)
            oT_sb = [AR.alloc(512, BF16) for _ in range(2)]
            t_atm, t_pu, t_pm, t_ssum, t_rswa, t_pb = (
                Trk(), Trk(), Trk(), Trk(), Trk(), Trk())
            t_ptA, t_ptB = Trk(), Trk()
            t_oT = [Trk(), Trk()]

            def sc_ap(h):
                bank = ps[3] if h < 2 else ps[4]
                off = (h % 2) * 256
                return bank[:, off:off + 256], t_sc34[0 if h < 2 else 1]

            p.step("vector", lambda E: E.memset(pb, 0.0), sem="DV",
                   writes=[t_pb])
            it = 0
            for g in range(p1_groups):
                wsl = g % 2
                p.dma("gpsimd", wqb[wsl].rearrange("p (k c) -> p k c", k=NK),
                      wqkv_in[g].rearrange("k p c -> p k c"), f"WQ{wsl}",
                      writes=[t_wq[wsl]])
                par0 = n_pre_ch % 2
                p.step("scalar", lambda E, _g=g, _p=par0: E.copy(
                    S_bf[_g][_p], S[_g][_p]), sem="AC",
                    reads=[t_S[g][par0]], writes=[t_Sbf[g][par0]])
                for c in range(n_pre_ch - 1, n_ch):
                    pr = it % 2
                    it += 1
                    sl = c % 2
                    ti = c - (n_pre_ch - 1)
                    j = c - n_pre_ch
                    is_halo = (c == n_pre_ch - 1)
                    B = KV[pr]
                    Q = QN[pr]
                    wv_ = wqb[wsl].rearrange("p (k c) -> p k c", k=NK)
                    p.dma("sync", lnp[sl].rearrange("p (k t) -> p k t", k=NK),
                          ln1T[c].rearrange("k p t -> p k t"), f"LP{sl}",
                          writes=[t_lnp[sl]])
                    fns = []
                    for k in range(NK):
                        fns.append(lambda E, _k=k, _s=sl, _w=wv_, _P=FKV[pr]:
                                   E.matmul(
                            _P, lnp[_s].rearrange(
                                "p (k t) -> p k t", k=NK)[:, _k, :],
                            _w[:, _k, 512:768],
                            start=(_k == 0), stop=(_k == NK - 1)))
                    p.step("tensor", fns, sem="PE",
                           reads=[t_lnp[sl], t_wq[wsl]], rw=[t_FK[pr]])
                    if not is_halo:
                        fns = []
                        for k in range(NK):
                            fns.append(lambda E, _k=k, _s=sl, _w=wv_,
                                       _P=FQ[pr]: E.matmul(
                                _P, lnp[_s].rearrange(
                                    "p (k t) -> p k t", k=NK)[:, _k, :],
                                _w[:, _k, 0:512],
                                start=(_k == 0), stop=(_k == NK - 1)))
                        p.step("tensor", fns, sem="PE",
                               reads=[t_lnp[sl], t_wq[wsl]], rw=[t_FQ[pr]])
                    emit_kv_math(B, pr, do_gla=(not is_halo), v_slot=sl)
                    if not is_halo:
                        transpose_to(B.ktT, B.ktil, B.t_ktil, B.t_ktT,
                                     f"TK{pr}", ("sync", "scalar")[pr])
                    p.step("vector", lambda E, _B=B, _t=ti, _P=FKV[pr]:
                           E.tensor_tensor(
                        _B.t1k, _P[:, 0:128], tblk_v[:, _t, 0, :],
                        op=ALU.mult),
                        sem="DV", reads=[t_tbl], rw=[t_FK[pr]],
                        writes=[B.t_t1k])
                    p.step("vector", [
                        lambda E, _B=B, _t=ti, _P=FKV[pr]: E.tensor_tensor(
                            _B.t2k[:, 0:64], _P[:, 64:128],
                            tblk_v[:, _t, 1, 0:64], op=ALU.mult),
                        lambda E, _B=B, _t=ti, _P=FKV[pr]: E.tensor_tensor(
                            _B.t2k[:, 64:128], _P[:, 0:64],
                            tblk_v[:, _t, 1, 64:128], op=ALU.mult),
                    ], sem="DV", reads=[t_tbl], rw=[t_FK[pr]],
                        writes=[B.t_t2k])
                    p.step("vector", lambda E, _B=B: E.tensor_tensor(
                        _B.krb, _B.t1k, _B.t2k, op=ALU.add),
                        sem="DV", reads=[B.t_t1k, B.t_t2k], writes=[B.t_krb])
                    transpose_to(krT[sl], B.krb, B.t_krb, t_krT[sl],
                                 f"TR{sl}", ("sync", "scalar")[sl])
                    if is_halo:
                        continue
                    for h in range(GRP):
                        p.step("scalar", lambda E, _Q=Q, _h=h, _P=FQ[pr]:
                               E.activation(
                            _Q.eq[:, _h * 128:(_h + 1) * 128],
                            _P[:, _h * 128:(_h + 1) * 128], AF.Exp,
                            accum_out=_Q.sigq[:, _h:_h + 1]), sem="AC",
                            rw=[t_FQ[pr]], writes=[Q.t_eq, Q.t_sigq])
                    p.step("vector", lambda E, _Q=Q: E.reciprocal(
                        _Q.recipq, _Q.sigq), sem="DV",
                        reads=[Q.t_sigq], writes=[Q.t_recipq])
                    p.step("vector", lambda E, _Q=Q, _B=B: E.tensor_tensor(
                        _Q.qhu.rearrange("p (h c) -> p h c", h=GRP),
                        _Q.eq.rearrange("p (h c) -> p h c", h=GRP),
                        _B.ep.unsqueeze(1).broadcast_to([128, GRP, 128]),
                        op=ALU.mult),
                        sem="DV", reads=[Q.t_eq, B.t_ep], writes=[Q.t_qhu])
                    for h in range(GRP):
                        p.step("vector", lambda E, _Q=Q, _h=h: E.tensor_scalar(
                            _Q.qhat[:, _h * 128:(_h + 1) * 128],
                            _Q.qhu[:, _h * 128:(_h + 1) * 128],
                            _Q.recipq[:, _h:_h + 1], None, op0=ALU.mult),
                            sem="DV", reads=[Q.t_qhu, Q.t_recipq],
                            writes=[Q.t_qhat])
                    p.step("vector", lambda E, _Q=Q, _t=ti, _P=FQ[pr]:
                           E.tensor_tensor(
                        _Q.qt1.rearrange("p (h c) -> p h c", h=GRP),
                        _P.rearrange("p (h c) -> p h c", h=GRP),
                        tblq_v[:, _t, 0, :].unsqueeze(1).broadcast_to(
                            [128, GRP, 128]), op=ALU.mult),
                        sem="DV", reads=[t_tbl], rw=[t_FQ[pr]],
                        writes=[Q.t_qt1])
                    p.step("vector", [
                        lambda E, _Q=Q, _t=ti, _P=FQ[pr]: E.tensor_tensor(
                            _Q.qt2.rearrange("p (h a c) -> p h a c", h=GRP,
                                             a=2)[:, :, 0, :],
                            _P.rearrange("p (h a c) -> p h a c", h=GRP,
                                         a=2)[:, :, 1, :],
                            tblq_v[:, _t, 1, 0:64].unsqueeze(1).broadcast_to(
                                [128, GRP, 64]), op=ALU.mult),
                        lambda E, _Q=Q, _t=ti, _P=FQ[pr]: E.tensor_tensor(
                            _Q.qt2.rearrange("p (h a c) -> p h a c", h=GRP,
                                             a=2)[:, :, 1, :],
                            _P.rearrange("p (h a c) -> p h a c", h=GRP,
                                         a=2)[:, :, 0, :],
                            tblq_v[:, _t, 1, 64:128].unsqueeze(1).broadcast_to(
                                [128, GRP, 64]), op=ALU.mult),
                    ], sem="DV", reads=[t_tbl], rw=[t_FQ[pr]],
                        writes=[Q.t_qt2])
                    p.step("vector", lambda E, _Q=Q: E.tensor_tensor(
                        _Q.qrb, _Q.qt1, _Q.qt2, op=ALU.add),
                        sem="DV", reads=[Q.t_qt1, Q.t_qt2], writes=[Q.t_qrb])
                    for h in range(GRP):
                        transpose_to(Q.qhT[:, h * 128:(h + 1) * 128],
                                     Q.qhat[:, h * 128:(h + 1) * 128],
                                     Q.t_qhat, Q.t_qhT, f"TH{pr}",
                                     ("sync", "scalar")[pr])
                        transpose_to(Q.qrT[:, h * 128:(h + 1) * 128],
                                     Q.qrb[:, h * 128:(h + 1) * 128],
                                     Q.t_qrb, Q.t_qrT, f"TQ{pr}",
                                     ("scalar", "sync")[pr])
                    p.step("tensor", lambda E, _B=B, _Q=Q: E.matmul(
                        ps[2][:, 0:512], _B.ktT, _Q.qhT, start=True,
                        stop=True), sem="PE",
                        reads=[B.t_ktT, Q.t_qhT], rw=[t_ps2])
                    p.step("vector", lambda E: E.tensor_tensor(
                        atm, ps[2][:, 0:512], trim, op=ALU.mult), sem="DV",
                        reads=[t_tbl], rw=[t_ps2], writes=[t_atm])
                    for h in range(GRP):
                        ap_sc, t_sc = sc_ap(h)
                        p.step("tensor", [
                            lambda E, _Q=Q, _h=h, _a=ap_sc, _s=sl: E.matmul(
                                _a[:, 64:128],
                                _Q.qrT[:, _h * 128:(_h + 1) * 128],
                                krT[1 - _s][:, 64:128], start=True,
                                stop=True),
                            lambda E, _Q=Q, _h=h, _a=ap_sc, _s=sl: E.matmul(
                                _a[:, 128:256],
                                _Q.qrT[:, _h * 128:(_h + 1) * 128],
                                krT[_s], start=True, stop=True),
                        ], sem="PE",
                            reads=[Q.t_qrT, t_krT[sl], t_krT[1 - sl]],
                            rw=[t_sc])
                    for h in range(GRP):
                        ap_sc, t_sc = sc_ap(h)
                        p.step("scalar", lambda E, _h=h, _a=ap_sc:
                               E.activation(
                            pu[:, _h * 192:(_h + 1) * 192], _a[:, 64:256],
                            AF.Exp), sem="AC", rw=[t_sc], writes=[t_pu])
                    mk = maskA if j == 0 else maskR
                    p.step("vector", lambda E, _m=mk: E.tensor_tensor(
                        pm, pu, _m, op=ALU.mult), sem="DV",
                        reads=[t_pu, t_tbl], writes=[t_pm])
                    p.step("vector", lambda E: E.tensor_reduce(
                        ssum4, pm.rearrange("p (h c) -> p h c", h=GRP),
                        axis=AX.X, op=ALU.add), sem="DV",
                        reads=[t_pm], writes=[t_ssum])
                    p.step("vector", lambda E: E.reciprocal(rswa, ssum4),
                           sem="DV", reads=[t_ssum], writes=[t_rswa])
                    for h in range(GRP):
                        p.step("vector", lambda E, _h=h: E.tensor_scalar(
                            pb[:, _h * 256 + 64:(_h + 1) * 256],
                            pm[:, _h * 192:(_h + 1) * 192],
                            rswa[:, _h:_h + 1], 0.5, op0=ALU.mult,
                            op1=ALU.mult),
                            sem="DV", reads=[t_pm, t_rswa], writes=[t_pb])
                    for h in range(GRP):
                        transpose_to(ptA[:, h * 128:(h + 1) * 128],
                                     pb[:, h * 256:h * 256 + 128],
                                     t_pb, t_ptA, "TA", "sync")
                        transpose_to(ptB[:, h * 128:(h + 1) * 128],
                                     pb[:, h * 256 + 128:(h + 1) * 256],
                                     t_pb, t_ptB, "TB", "scalar")
                    par = c % 2
                    mmops = []
                    if o_parts in ("both", "gla"):
                        mmops.append((Vring[sl], atm, None, t_atm))
                        mmops.append((S_bf[g][par], Q.qhT, None, Q.t_qhT))
                    if o_parts in ("both", "swa"):
                        for h in range(GRP):
                            mmops.append((Vring[1 - sl][64:128, :],
                                          ptA[64:128,
                                              h * 128:(h + 1) * 128],
                                          (h * 128, (h + 1) * 128), t_ptA))
                            mmops.append((Vring[sl],
                                          ptB[:, h * 128:(h + 1) * 128],
                                          (h * 128, (h + 1) * 128), t_ptB))
                    ofns = []
                    for i2, (lh, rh, cols, _) in enumerate(mmops):
                        outap = (ps[5][:, :] if cols is None
                                 else ps[5][:, cols[0]:cols[1]])
                        ofns.append(lambda E, _l=lh, _r=rh, _o=outap,
                                    _s=(i2 == 0), _p=(i2 == len(mmops) - 1):
                                    E.matmul(_o, _l, _r, start=_s, stop=_p))
                    p.step("tensor", ofns, sem="PE",
                           reads=[t_V[sl], t_V[1 - sl], t_atm,
                                  t_Sbf[g][par], Q.t_qhT, t_ptA, t_ptB],
                           rw=[t_ps5])
                    p.step("scalar", lambda E, _s=sl: E.copy(
                        oT_sb[_s], ps[5][:, :]), sem="AC",
                        rw=[t_ps5], writes=[t_oT[sl]])
                    p.dma("sync", oT_dram[j, g], oT_sb[sl], f"OS{sl}",
                          reads=[t_oT[sl]])
                    emit_state_update(B, pr, g, (c + 1) % 2, sl)
                    p.step("scalar", lambda E, _g=g, _p=(c + 1) % 2: E.copy(
                        S_bf[_g][_p], S[_g][_p]), sem="AC",
                        reads=[t_S[g][(c + 1) % 2]],
                        writes=[t_Sbf[g][(c + 1) % 2]])
            AR.pop()

        if "ppre" in phases or "p1" in phases:
            AR.pop()

        ntok = n_own_ch * 128
        nts = ntok // 512

        if "p2" in phases:
            p.full_barrier()
            AR.push()
            oTv = AR.alloc(n_own_ch * NKV * 512, BF16).rearrange(
                "p (j g c) -> p j g c", j=n_own_ch, g=NKV)
            t_oTr = Trk()
            for j_ in range(n_own_ch):
                p.dma("gpsimd", oTv[:, j_], oT_dram[j_].rearrange(
                    "g p c -> p g c"), "OTL", writes=[t_oTr])
            wot = [AR.alloc(NK * 128, BF16) for _ in range(2)]
            t_wot = [Trk(), Trk()]
            xts = [AR.alloc(512, F32) for _ in range(2)]
            t_xts = [Trk(), Trk()]
            x1b = [AR.alloc(512, BF16) for _ in range(2)]
            t_x1b = [Trk(), Trk()]
            t_ps0 = Trk()
            for jc in range(NK):
                wsl = jc % 2
                p.dma("gpsimd", wot[wsl].rearrange("p (h c) -> p h c", h=NK),
                      wo_in[jc].rearrange("h p c -> p h c"), f"WO{wsl}",
                      writes=[t_wot[wsl]])
                for ts in range(nts):
                    xsl = ts % 2
                    j0 = ts * 4
                    p.dma("sync", xts[xsl].rearrange("p (c t) -> p c t", c=4),
                          xT_in[n_pre_ch + j0:n_pre_ch + j0 + 4, jc].rearrange(
                              "c p t -> p c t"), f"XS{xsl}",
                          writes=[t_xts[xsl]])
                    fns = []
                    for hd in range(NK):
                        gg, hh = hd // GRP, hd % GRP
                        fns.append(lambda E, _h=hd, _g=gg, _hh=hh, _j=j0,
                                   _w=wsl: E.matmul(
                            ps[0][:, :],
                            wot[_w].rearrange("p (h c) -> p h c", h=NK)[:, _h, :],
                            oTv[:, _j:_j + 4, _g,
                                _hh * 128:(_hh + 1) * 128],
                            start=(_h == 0), stop=(_h == NK - 1)))
                    p.step("tensor", fns, sem="PE",
                           reads=[t_oTr, t_wot[wsl]], writes=[t_ps0])
                    p.step("vector", lambda E, _x=xsl, _b=xsl: E.tensor_tensor(
                        x1b[_b], ps[0][:, :], xts[_x], op=ALU.add),
                        sem="DV", reads=[t_xts[xsl]], rw=[t_ps0],
                        writes=[t_x1b[xsl]])
                    p.dma("sync", x1T_dram[jc, :, ts * 512:(ts + 1) * 512],
                          x1b[xsl], f"XW{xsl}", reads=[t_x1b[xsl]])
            AR.pop()

        if "p3" in phases:
            p.full_barrier()
            AR.push()
            ln2T = AR.alloc(NK * ntok, BF16).rearrange(
                "p (k t) -> p k t", k=NK)
            t_ln2 = Trk()
            AR.push()
            x1s = [AR.alloc(ntok, BF16) for _ in range(2)]
            t_x1s = [Trk(), Trk()]
            x2s = AR.alloc(ntok, BF16)
            t_x2s = Trk()
            srow2 = AR.alloc(2 * ntok, F32)
            m2row = AR.alloc(ntok, F32)
            r2row = AR.alloc(ntok, F32)
            t_sr2, t_m2, t_r2 = Trk(), Trk(), Trk()
            rep2_m = AR.alloc(ntok, F32)
            rep2_r = AR.alloc(ntok, F32)
            t_rep2 = Trk()
            t_pst = [Trk() for _ in range(8)]
            for jc in range(NK):
                xsl = jc % 2
                p.dma("sync", x1s[xsl], x1T_dram[jc], f"X1{xsl}",
                      writes=[t_x1s[xsl]])
                p.step("scalar", lambda E, _x=xsl: E.activation(
                    x2s, x1s[_x], AF.Square), sem="AC",
                    reads=[t_x1s[xsl]], writes=[t_x2s])
                for ts in range(nts):
                    st_, sp_ = (jc == 0), (jc == NK - 1)
                    p.step("tensor", [
                        lambda E, _x=xsl, _t=ts, _s=st_, _p=sp_: E.matmul(
                            ps[_t][0:1, :], ones_col_b,
                            x1s[_x][:, _t * 512:(_t + 1) * 512],
                            start=_s, stop=_p),
                        lambda E, _t=ts, _s=st_, _p=sp_: E.matmul(
                            ps[4 + _t][0:1, :], ones_col_b,
                            x2s[:, _t * 512:(_t + 1) * 512],
                            start=_s, stop=_p),
                    ], sem="PE", reads=[t_x1s[xsl], t_x2s, t_cst],
                        writes=[t_pst[ts], t_pst[4 + ts]])
            for ts in range(nts):
                p.step("scalar", [
                    lambda E, _t=ts: E.copy(
                        srow2[0:1, _t * 512:(_t + 1) * 512], ps[_t][0:1, :]),
                    lambda E, _t=ts: E.copy(
                        srow2[0:1, ntok + _t * 512:ntok + (_t + 1) * 512],
                        ps[4 + _t][0:1, :]),
                ], sem="AC", rw=[t_pst[ts], t_pst[4 + ts]], writes=[t_sr2])
            p.step("vector", lambda E: E.tensor_scalar(
                m2row[0:1, :], srow2[0:1, 0:ntok], 1.0 / D, None, op0=ALU.mult),
                sem="DV", reads=[t_sr2], writes=[t_m2])
            p.step("vector", lambda E: E.tensor_scalar(
                r2row[0:1, :], srow2[0:1, ntok:2 * ntok], 1.0 / D, None,
                op0=ALU.mult), sem="DV", reads=[t_sr2], writes=[t_r2])
            p.step("scalar", lambda E: E.activation(
                srow2[0:1, 0:ntok], m2row[0:1, :], AF.Square), sem="AC",
                reads=[t_m2], writes=[t_sr2])
            p.step("vector", lambda E: E.tensor_tensor(
                r2row[0:1, :], r2row[0:1, :], srow2[0:1, 0:ntok],
                op=ALU.subtract), sem="DV", reads=[t_sr2], rw=[t_r2])
            p.step("scalar", lambda E: E.activation(
                r2row[0:1, :], r2row[0:1, :], AF.Ln, bias=eps_col[0:1, :]),
                sem="AC", rw=[t_r2])
            p.step("scalar", lambda E: E.activation(
                r2row[0:1, :], r2row[0:1, :], AF.Exp, scale=-0.5),
                sem="AC", rw=[t_r2])
            for piece in range(nts):
                sl2 = slice(piece * 512, (piece + 1) * 512)
                p.step("tensor", [
                    lambda E, _s=sl2: E.matmul(ps[0][:, :], ones_row_f[0:1, :],
                                               m2row[0:1, _s], start=True,
                                               stop=True),
                    lambda E, _s=sl2: E.matmul(ps[1][:, :], ones_row_f[0:1, :],
                                               r2row[0:1, _s], start=True,
                                               stop=True),
                ], sem="PE", reads=[t_m2, t_r2, t_cst],
                    writes=[t_pst[0], t_pst[1]])
                p.step("scalar", [
                    lambda E, _s=sl2: E.copy(rep2_m[:, _s], ps[0][:, :]),
                    lambda E, _s=sl2: E.copy(rep2_r[:, _s], ps[1][:, :]),
                ], sem="AC", rw=[t_pst[0], t_pst[1]], writes=[t_rep2])
            for jc in range(NK):
                xsl = jc % 2
                p.dma("sync", x1s[xsl], x1T_dram[jc], f"X1{xsl}",
                      writes=[t_x1s[xsl]])
                p.step("vector", lambda E, _x=xsl: E.tensor_tensor(
                    x1s[_x], x1s[_x], rep2_m, op=ALU.subtract),
                    sem="DV", reads=[t_rep2], rw=[t_x1s[xsl]])
                p.step("vector", lambda E, _x=xsl, _j=jc: E.tensor_tensor(
                    ln2T[:, _j, :], x1s[_x], rep2_r, op=ALU.mult),
                    sem="DV", reads=[t_rep2, t_x1s[xsl]], writes=[t_ln2])
                if ln2dbg is not None:
                    p.dma("sync", ln2dbg[jc], ln2T[:, jc, :], "L2D",
                          reads=[t_ln2])
            AR.pop()

        if "p4" in phases:
            p.full_barrier()
            wrt = AR.alloc(NK * 8, BF16).rearrange("p (k e) -> p k e", k=NK)
            att = AR.alloc(NK * 128, BF16).rearrange("p (k c) -> p k c", k=NK)
            bcol = AR.alloc(NK, F32)
            t_wrt, t_att, t_bcol = Trk(), Trk(), Trk()
            p.dma("gpsimd", wrt, wr_in.rearrange("k p e -> p k e"), "P4C",
                  writes=[t_wrt])
            p.dma("gpsimd", att, at_in.rearrange("k p c -> p k c"), "P4D",
                  writes=[t_att])
            p.dma("sync", bcol, b_in, "P4B", writes=[t_bcol])
            lgts = AR.alloc(8, F32)
            m1c = AR.alloc(1, F32)
            mk1 = AR.alloc(8, F32)
            tm8 = AR.alloc(8, F32)
            m2c = AR.alloc(1, F32)
            mk2 = AR.alloc(8, F32)
            w1c = AR.alloc(1, F32)
            w2c = AR.alloc(1, F32)
            wfld = AR.alloc(8, F32)
            hw = AR.alloc(128, BF16)
            hwT = AR.alloc(ntok, BF16)
            t_lg, t_m1c, t_mk1, t_tm8, t_m2c, t_mk2 = (
                Trk(), Trk(), Trk(), Trk(), Trk(), Trk())
            t_w1, t_w2, t_wf, t_hw, t_hwT = Trk(), Trk(), Trk(), Trk(), Trk()
            t_psr = Trk()
            t_psh = Trk()
            for tc in range(n_own_ch):
                tsl = slice(tc * 128, (tc + 1) * 128)
                fns = [lambda E, _k=k2, _t=tc: E.matmul(
                    ps[2][:, 0:8], ln2T[:, _k, _t * 128:(_t + 1) * 128],
                    wrt[:, _k, :], start=(_k == 0), stop=(_k == NK - 1))
                    for k2 in range(NK)]
                p.step("tensor", fns, sem="PE", reads=[t_ln2, t_wrt],
                       writes=[t_psr])
                p.step("vector", lambda E: E.tensor_copy(lgts, ps[2][:, 0:8]),
                       sem="DV", rw=[t_psr], writes=[t_lg])
                p.step("vector", lambda E: E.tensor_reduce(
                    m1c, lgts, axis=AX.X, op=ALU.max), sem="DV",
                    reads=[t_lg], writes=[t_m1c])
                p.step("vector", lambda E: E.tensor_tensor(
                    mk1, lgts, m1c.broadcast_to([128, 8]), op=ALU.is_ge),
                    sem="DV", reads=[t_lg, t_m1c], writes=[t_mk1])
                p.step("vector", lambda E: E.tensor_scalar(
                    tm8, mk1, -1e30, None, op0=ALU.mult), sem="DV",
                    reads=[t_mk1], writes=[t_tm8])
                p.step("vector", lambda E: E.tensor_tensor(
                    tm8, lgts, tm8, op=ALU.add), sem="DV",
                    reads=[t_lg], rw=[t_tm8])
                p.step("vector", lambda E: E.tensor_reduce(
                    m2c, tm8, axis=AX.X, op=ALU.max), sem="DV",
                    reads=[t_tm8], writes=[t_m2c])
                p.step("vector", lambda E: E.tensor_tensor(
                    mk2, tm8, m2c.broadcast_to([128, 8]), op=ALU.is_ge),
                    sem="DV", reads=[t_tm8, t_m2c], writes=[t_mk2])
                p.step("vector", lambda E: E.tensor_tensor(
                    w1c, m2c, m1c, op=ALU.subtract), sem="DV",
                    reads=[t_m1c, t_m2c], writes=[t_w1])
                p.step("scalar", lambda E: E.activation(
                    w1c, w1c, AF.Exp), sem="AC", rw=[t_w1])
                p.step("vector", lambda E: E.tensor_scalar(
                    w1c, w1c, 1.0, None, op0=ALU.add), sem="DV", rw=[t_w1])
                p.step("vector", lambda E: E.reciprocal(w1c, w1c),
                       sem="DV", rw=[t_w1])
                p.step("vector", lambda E: E.tensor_scalar(
                    w1c, w1c, 2.0, None, op0=ALU.mult), sem="DV", rw=[t_w1])
                p.step("vector", lambda E: E.tensor_scalar(
                    w2c, w1c, -1.0, 2.0, op0=ALU.mult, op1=ALU.add),
                    sem="DV", reads=[t_w1], writes=[t_w2])
                p.step("vector", lambda E: E.tensor_scalar(
                    wfld, mk1, w1c[:, 0:1], None, op0=ALU.mult), sem="DV",
                    reads=[t_mk1, t_w1], writes=[t_wf])
                p.step("vector", lambda E: E.tensor_scalar(
                    tm8, mk2, w2c[:, 0:1], None, op0=ALU.mult), sem="DV",
                    reads=[t_mk2, t_w2], rw=[t_tm8])
                p.step("vector", lambda E: E.tensor_tensor(
                    wfld, wfld, tm8, op=ALU.add), sem="DV",
                    reads=[t_tm8], rw=[t_wf])
                fns = [lambda E, _k=k2, _t=tc: E.matmul(
                    ps[3][:, 0:128], ln2T[:, _k, _t * 128:(_t + 1) * 128],
                    att[:, _k, :], start=(_k == 0), stop=(_k == NK - 1))
                    for k2 in range(NK)]
                p.step("tensor", fns, sem="PE", reads=[t_ln2, t_att],
                       writes=[t_psh])
                p.step("vector", lambda E: E.tensor_tensor(
                    hw.rearrange("p (e r) -> p e r", e=8),
                    ps[3][:, 0:128].rearrange("p (e r) -> p e r", e=8),
                    wfld.unsqueeze(2).broadcast_to([128, 8, 16]),
                    op=ALU.mult), sem="DV", reads=[t_wf], rw=[t_psh],
                    writes=[t_hw])
                p.step("scalar", lambda E, _t=tc: E.dma_start(
                    out=hwT[:, _t * 128:(_t + 1) * 128], in_=hw,
                    transpose=True), sem="THW", inc=16,
                    reads=[t_hw], writes=[t_hwT], dma=True)
            wbt = [AR.alloc(NK * 128, BF16) for _ in range(2)]
            bft = [AR.alloc(128, BF16) for _ in range(2)]
            t_wbt = [Trk(), Trk()]
            t_bft = [Trk(), Trk()]
            x1r = [AR.alloc(512, BF16) for _ in range(2)]
            t_x1r = [Trk(), Trk()]
            of32 = [AR.alloc(512, F32) for _ in range(2)]
            t_of = [Trk(), Trk()]
            og32 = [AR.alloc(512, F32) for _ in range(2)]
            t_og = [Trk(), Trk()]
            t_ps45 = Trk()
            t_ps67 = [Trk(), Trk()]
            for oc in range(NK):
                wsl = oc % 2
                p.dma("gpsimd", wbt[wsl].rearrange("p (k c) -> p k c", k=NK),
                      wbT_in[oc].rearrange("k p c -> p k c"), f"WB{wsl}",
                      writes=[t_wbt[wsl]])
                p.dma("gpsimd", bft[wsl], bf_in[oc], f"BF{wsl}",
                      writes=[t_bft[wsl]])
                for ts in range(nts):
                    xsl = ts % 2
                    fns = [lambda E, _k=k2, _t=ts, _w=wsl: E.matmul(
                        ps[4][:, :],
                        wbt[_w].rearrange("p (k c) -> p k c", k=NK)[:, _k, :],
                        ln2T[:, _k, _t * 512:(_t + 1) * 512],
                        start=(_k == 0), stop=False)
                        for k2 in range(NK)]
                    fns.append(lambda E, _t=ts, _w=wsl: E.matmul(
                        ps[4][:, :], bft[_w],
                        hwT[:, _t * 512:(_t + 1) * 512],
                        start=False, stop=True))
                    p.step("tensor", fns, sem="PE",
                           reads=[t_ln2, t_wbt[wsl], t_bft[wsl], t_hwT],
                           writes=[t_ps45])
                    p.dma("sync", x1r[xsl],
                          x1T_dram[oc, :, ts * 512:(ts + 1) * 512],
                          f"XR{xsl}", writes=[t_x1r[xsl]])
                    p.step("vector", lambda E, _o=oc, _x=xsl: E.tensor_scalar(
                        of32[_x], ps[4][:, :], bcol[:, _o:_o + 1], None,
                        op0=ALU.add), sem="DV", rw=[t_ps45],
                        reads=[t_bcol], writes=[t_of[xsl]])
                    p.step("vector", lambda E, _x=xsl: E.tensor_tensor(
                        og32[_x], of32[_x], x1r[_x], op=ALU.add),
                        sem="DV", reads=[t_of[xsl], t_x1r[xsl]],
                        writes=[t_og[xsl]])
                    for q4 in range(4):
                        pb_ = 6 + (q4 % 2)
                        p.step("tensor", lambda E, _q=q4, _x=xsl, _b=pb_:
                               E.transpose(ps[_b][:, 0:128],
                                           og32[_x][:, _q * 128:(_q + 1) * 128],
                                           ident_f), sem="PE",
                               reads=[t_og[xsl], t_cst],
                               rw=[t_ps67[q4 % 2]])
                        p.step("scalar", lambda E, _b=pb_, _x=xsl, _q=q4:
                               E.copy(of32[_x][:, _q * 128:(_q + 1) * 128],
                                      ps[_b][:, 0:128]), sem="AC",
                               rw=[t_ps67[q4 % 2]], writes=[t_of[xsl]])
                    p.dma("sync",
                          out_hbm[ts * 512:(ts + 1) * 512,
                                  oc * 128:(oc + 1) * 128].rearrange(
                              "(a t) f -> t a f", a=4),
                          of32[xsl].rearrange("p (a f) -> p a f", a=4),
                          f"OW{xsl}", reads=[t_of[xsl]])
            AR.pop()

        p.full_barrier()

        sems = {}
        for name in sorted(p.ctr):
            sems[name] = E(nc.semaphore(name))
        block = E(nc.Block())

        def run_stream(eng_name):
            def body(Eng):
                Eng._sems = sems
                for fn in p.streams[eng_name]:
                    fn(Eng)
            return body

        block.sync(run_stream("sync"))
        block.gpsimd(run_stream("gpsimd"))
        block.tensor(run_stream("tensor"))
        block.vector(run_stream("vector"))
        block.scalar(run_stream("scalar"))

    return nc



def make_consts():
    c = np.zeros((8, 128, 128), np.float32)
    c[0] = np.eye(128, dtype=np.float32)
    c[1] = (np.tril(np.ones((128, 128), np.float32)) / GATE_NORM).T
    c[2][127, 0] = 1.0
    c[3][:] = 1.0
    c[4][:] = EPS
    return c


def make_xT_tiled(x_core):
    tok = x_core.shape[0]
    n_ch = tok // 128
    t = x_core.reshape(n_ch, 128, NK, 128)
    return np.ascontiguousarray(t.transpose(0, 2, 3, 1))


def tile_wqkv(Wq, Wk, Wv):
    out = np.zeros((NKV, NK, 128, 768), np.float32)
    for g in range(NKV):
        cat = np.concatenate([Wq[:, g * 512:(g + 1) * 512],
                              Wk[:, g * 128:(g + 1) * 128],
                              Wv[:, g * 128:(g + 1) * 128]], axis=1)
        out[g] = cat.reshape(NK, 128, 768)
    return out


def make_rope_tables(n_sw, pos0):
    inv = 1.0 / (10000.0 ** (2.0 * np.arange(64) / 128.0))
    ropek = np.zeros((n_sw, 2, 128, 128), np.float32)
    for s in range(n_sw):
        pos = pos0 + s * 128 + np.arange(128)
        ang = pos[:, None].astype(np.float64) * inv[None, :]
        c = np.cos(ang)
        si = np.sin(ang)
        ropek[s, 0, :, 0:64] = c
        ropek[s, 0, :, 64:128] = c
        ropek[s, 1, :, 0:64] = -si
        ropek[s, 1, :, 64:128] = si
    ropeq = (ropek * INV_SQRT_D).astype(np.float32)
    return ropeq, ropek


def make_masks(first):
    t = np.arange(128)[:, None]
    w = np.arange(192)[None, :]
    m = (w >= t) & (w <= t + 64)
    if first:
        m = m & (w >= 64)
    m = m.astype(np.float32)
    return np.tile(m, (1, 4))


def make_trimask():
    s = np.arange(128)[:, None]
    t = np.arange(128)[None, :]
    m = 0.5 * (s <= t).astype(np.float32)
    return np.tile(m, (1, 4))


_BUILD_CACHE = {}


def _get_program():
    key = "full"
    if key not in _BUILD_CACHE:
        _BUILD_CACHE[key] = build_program(
            phases=("p0", "ppre", "p1", "p2", "p3", "p4"))
    return _BUILD_CACHE[key]


def kernel(L0_attn_hidden_states, L0_attn_Wq, L0_attn_Wk, L0_attn_Wv,
           L0_attn_Wo, L0_ffn_W_base, L0_ffn_b_base, L0_ffn_W_router,
           L0_ffn_A, L0_ffn_B):
    x = np.asarray(L0_attn_hidden_states)
    Wq = np.asarray(L0_attn_Wq)
    Wk = np.asarray(L0_attn_Wk)
    Wv = np.asarray(L0_attn_Wv)
    Wo = np.asarray(L0_attn_Wo)
    Wb = np.asarray(L0_ffn_W_base)
    bb = np.asarray(L0_ffn_b_base)
    Wr = np.asarray(L0_ffn_W_router)
    A = np.asarray(L0_ffn_A)
    B = np.asarray(L0_ffn_B)

    wqkv = tile_wqkv(Wq, Wk, Wv)
    wo_t = np.ascontiguousarray(
        Wo.reshape(NK, 128, NK, 128).transpose(2, 0, 1, 3))
    wbT_t = np.ascontiguousarray(
        Wb.T.reshape(NK, 128, NK, 128).transpose(2, 0, 1, 3))
    wr_t = np.ascontiguousarray(Wr.T.reshape(NK, 128, 8))
    at_t = np.ascontiguousarray(
        A.reshape(128, D).T.reshape(NK, 128, 128))
    bf_t = np.ascontiguousarray(
        B.transpose(0, 2, 1).reshape(128, D).reshape(128, NK, 128)
        .transpose(1, 0, 2))
    b_t = np.ascontiguousarray(bb.reshape(NK, 128).T)
    consts = make_consts()
    maskR = make_masks(False)
    maskAf = make_masks(True)
    trim = make_trimask()

    nc = _get_program()
    in_maps = []
    for core in range(N_CORES):
        s, pp = core // 4, core % 4
        own0 = pp * OWN_TOK
        lo = own0 - N_PRE_CH * 128
        xc = np.zeros((TOK_ALL, D), np.float32)
        src0 = max(lo, 0)
        xc[src0 - lo:N_PRE_CH * 128 + OWN_TOK] = x[s, src0:own0 + OWN_TOK]
        ropeq, ropek = make_rope_tables(N_OWN_CH + 1, own0 - 128)
        in_maps.append({
            "xT_in": make_xT_tiled(xc),
            "consts": consts,
            "wqkv_in": wqkv,
            "ropeq_in": ropeq,
            "ropek_in": ropek,
            "maskR_in": maskR,
            "maskA_in": maskAf if pp == 0 else maskR,
            "trimask_in": trim,
            "wo_in": wo_t,
            "wbT_in": wbT_t,
            "wr_in": wr_t,
            "at_in": at_t,
            "bf_in": bf_t,
            "b_in": b_t,
        })

    res = run_bass_kernel_spmd(nc, in_maps, core_ids=list(range(N_CORES)))
    out = np.empty((2, 8192, D), np.float32)
    for core in range(N_CORES):
        s, pp = core // 4, core % 4
        out[s, pp * OWN_TOK:(pp + 1) * OWN_TOK] = res.results[core]["out_hbm"]
    return out


# version 05 (final)
# speedup vs baseline: 1.0306x; 1.0306x over previous
import numpy as np
import ml_dtypes
import concourse.bass as bass
import concourse.mybir as mybir
from concourse.bass_utils import run_bass_kernel_spmd
from contextlib import ExitStack

F32 = mybir.dt.float32
BF16 = mybir.dt.bfloat16
AF = mybir.ActivationFunctionType
ALU = mybir.AluOpType
AX = mybir.AxisListType
BF = ml_dtypes.bfloat16

D = 4096
NK = D // 128
NH, NKV, HD = 32, 8, 128
GRP = NH // NKV
EPS = 1e-5
GATE_NORM = 16.0
INV_SQRT_D = 1.0 / np.sqrt(128.0)

N_CORES = 8
OWN_TOK = 2048
N_OWN_CH = OWN_TOK // 128
N_PRE_CH = 48
TOK_ALL = (N_PRE_CH + N_OWN_CH) * 128
CB = 16

SAFE_CHAIN = {"tensor"}



ENGINES = ("sync", "gpsimd", "tensor", "vector", "scalar")


class Trk:
    __slots__ = ("ready", "users")

    def __init__(self):
        self.ready = {}
        self.users = {}


class Prog:
    def __init__(self):
        self.streams = {e: [] for e in ENGINES}
        self.ctr = {}
        self.water = {e: {} for e in ENGINES}

    def _bump(self, sem, k):
        self.ctr[sem] = self.ctr.get(sem, 0) + k
        return self.ctr[sem]

    def _wait(self, eng, sem, val):
        if val <= 0:
            return
        if self.water[eng].get(sem, 0) >= val:
            return
        self.water[eng][sem] = val
        self.streams[eng].append(
            lambda E, _s=sem, _v=val: E.wait_ge(E._sems[_s], _v))

    def barrier_from(self, eng, sems):
        for s in sems:
            self._wait(eng, s, self.ctr.get(s, 0))

    def full_barrier(self):
        for e in ENGINES:
            for s in list(self.ctr):
                self._wait(e, s, self.ctr[s])

    def step(self, eng, fns, sem=None, inc=1, reads=(), writes=(), rw=(),
             dma=False):
        if not isinstance(fns, (list, tuple)):
            fns = [fns]
        reads = list(reads) + list(rw)
        writes = list(writes) + list(rw)
        for r in reads:
            for s, (v, peng) in r.ready.items():
                if peng == eng and eng in SAFE_CHAIN and not dma:
                    continue
                self._wait(eng, s, v)
        for w in writes:
            for s, v in w.users.items():
                self._wait(eng, s, v)
            for s, (v, peng) in w.ready.items():
                if peng == eng and eng in SAFE_CHAIN and not dma:
                    continue
                self._wait(eng, s, v)
        for fn in fns[:-1]:
            self.streams[eng].append(fn)
        if sem is None:
            self.streams[eng].append(fns[-1])
            val = None
        else:
            val = self._bump(sem, inc)
            self.streams[eng].append(
                lambda E, _f=fns[-1], _s=sem, _k=inc: _f(E).then_inc(E._sems[_s], _k))
            for r in reads:
                r.users[sem] = val
            for w in writes:
                w.ready[sem] = (val, None if dma else eng)
                w.users = {}
        return val

    def dma(self, eng, out, in_, sem, reads=(), writes=(), **kw):
        return self.step(
            eng, lambda E: E.dma_start(out=out, in_=in_, **kw),
            sem=sem, inc=16, reads=reads, writes=writes, dma=True)


class Arena:

    def __init__(self, nc, name, nbytes):
        self.t = None
        self.nc = nc
        self.name = name
        self.nbytes = nbytes
        self.off = 0
        self.marks = []

    def open(self, ctx):
        self.t = ctx.enter_context(
            self.nc.sbuf_tensor(self.name, [128, self.nbytes], mybir.dt.uint8))
        return self

    def alloc(self, cols, dtype, ncopies=1):
        es = mybir.dt.size(dtype)
        out = []
        for _ in range(ncopies):
            off = (self.off + 63) & ~63
            self.off = off + cols * es
            assert self.off <= self.nbytes, (
                f"arena {self.name} overflow: {self.off} > {self.nbytes}")
            ap = self.t.ap()[:, off:off + cols * es].bitcast(dtype)
            out.append(ap)
        return out[0] if ncopies == 1 else out

    def push(self):
        self.marks.append(self.off)

    def pop(self):
        self.off = self.marks.pop()



def build_program(n_pre_ch=N_PRE_CH, n_own_ch=N_OWN_CH, phases=("p0",),
                  debug_taps=(), p1_level=99, p1_groups=NKV, o_parts="both"):
    n_ch = n_pre_ch + n_own_ch
    cb = min(CB, n_ch)
    assert n_ch % cb == 0
    n_batch = n_ch // cb

    nc = bass.Bass("TRN2", target_bir_lowering=False, debug=False)

    xT_in = nc.dram_tensor("xT_in", [n_ch, NK, 128, 128], F32,
                           kind="ExternalInput").ap()
    consts = nc.dram_tensor("consts", [8, 128, 128], F32,
                            kind="ExternalInput").ap()
    n_sw = n_own_ch + 1
    wqkv_in = ropeq_in = ropek_in = maskR_in = maskA_in = trimask_in = None
    if "ppre" in phases or "p1" in phases:
        wqkv_in = nc.dram_tensor("wqkv_in", [NKV, NK, 128, 768], F32,
                                 kind="ExternalInput").ap()
    if "p1" in phases:
        ropeq_in = nc.dram_tensor("ropeq_in", [n_sw, 2, 128, 128], F32,
                                  kind="ExternalInput").ap()
        ropek_in = nc.dram_tensor("ropek_in", [n_sw, 2, 128, 128], F32,
                                  kind="ExternalInput").ap()
        maskR_in = nc.dram_tensor("maskR_in", [128, 768], F32,
                                  kind="ExternalInput").ap()
        maskA_in = nc.dram_tensor("maskA_in", [128, 768], F32,
                                  kind="ExternalInput").ap()
        trimask_in = nc.dram_tensor("trimask_in", [128, 512], F32,
                                    kind="ExternalInput").ap()

    wo_in = wbT_in = wr_in = at_in = bf_in = b_in = None
    if "p2" in phases:
        wo_in = nc.dram_tensor("wo_in", [NK, NK, 128, 128], F32,
                               kind="ExternalInput").ap()
    if "p4" in phases:
        wbT_in = nc.dram_tensor("wbT_in", [NK, NK, 128, 128], F32,
                                kind="ExternalInput").ap()
        wr_in = nc.dram_tensor("wr_in", [NK, 128, 8], F32,
                               kind="ExternalInput").ap()
        at_in = nc.dram_tensor("at_in", [NK, 128, 128], F32,
                               kind="ExternalInput").ap()
        bf_in = nc.dram_tensor("bf_in", [NK, 128, 128], F32,
                               kind="ExternalInput").ap()
        b_in = nc.dram_tensor("b_in", [128, NK], F32,
                              kind="ExternalInput").ap()

    ln1T = nc.dram_tensor(
        "ln1T", [n_ch, NK, 128, 128], BF16,
        kind="ExternalOutput" if "ln1T" in debug_taps else "Internal").ap()
    oT_dram = None
    if "p1" in phases:
        oT_dram = nc.dram_tensor(
            "oT_dram", [n_own_ch, NKV, 128, 512], BF16,
            kind="ExternalOutput" if "oT" in debug_taps else "Internal").ap()
    dbg_bf = dbg_f32 = None
    if "gdbg" in debug_taps:
        dbg_bf = nc.dram_tensor("dbg_bf", [4, 128, 512], BF16,
                                kind="ExternalOutput").ap()
        dbg_f32 = nc.dram_tensor("dbg_f32", [2, 128, 512], F32,
                                 kind="ExternalOutput").ap()
    x1T_dram = None
    if "p2" in phases:
        x1T_dram = nc.dram_tensor(
            "x1T_dram", [NK, 128, n_own_ch * 128], BF16,
            kind="ExternalOutput" if "x1T" in debug_taps else "Internal").ap()
    out_hbm = None
    if "p4" in phases:
        out_hbm = nc.dram_tensor("out_hbm", [n_own_ch * 128, D], F32,
                                 kind="ExternalOutput").ap()
    ln2dbg = None
    if "ln2T" in debug_taps:
        ln2dbg = nc.dram_tensor("ln2dbg", [NK, 128, n_own_ch * 128], BF16,
                                kind="ExternalOutput").ap()
    sin_out = None
    if "Sin" in debug_taps:
        sin_out = nc.dram_tensor("sin_out", [NKV, 128, 128], F32,
                                 kind="ExternalOutput").ap()

    p = Prog()

    with ExitStack() as ctx:
        E = ctx.enter_context
        AR = Arena(nc, "arena", 205 * 1024).open(ctx)

        ident_f = AR.alloc(128, F32)
        tri16 = AR.alloc(128, F32)
        onehot = AR.alloc(1, F32)
        ones_col_f = AR.alloc(1, F32)
        ones_row_f = AR.alloc(128, F32)
        ones_col_b = AR.alloc(1, BF16)
        ident_b = AR.alloc(128, BF16)
        eps_col = AR.alloc(1, F32)

        t_cst = Trk()
        p.dma("sync", ident_f, consts[0], "CST", writes=[t_cst])
        p.dma("sync", tri16, consts[1], "CST", writes=[t_cst])
        p.dma("sync", onehot, consts[2, :, 0:1], "CST", writes=[t_cst],
              allow_slow_non_contiguous=True)
        p.dma("sync", ones_col_f, consts[3, :, 0:1], "CST", writes=[t_cst],
              allow_slow_non_contiguous=True)
        p.dma("sync", ones_row_f[0:1, :], consts[3, 0:1, :], "CST", writes=[t_cst])
        p.dma("gpsimd", ones_col_b, consts[3, :, 0:1], "CSG", writes=[t_cst],
              allow_slow_non_contiguous=True)
        p.dma("sync", eps_col, consts[4, :, 0:1], "CST", writes=[t_cst],
              allow_slow_non_contiguous=True)
        p.dma("gpsimd", ident_b, consts[0], "CSG", writes=[t_cst])

        AR.push()
        xbf = [AR.alloc(512, BF16) for _ in range(2)]
        x2b = [AR.alloc(512, BF16) for _ in range(2)]
        xf32 = [AR.alloc(cb * 128, F32) for _ in range(2)]
        lnout = [AR.alloc(cb * 128, BF16) for _ in range(2)]
        srow = AR.alloc(2 * cb * 128, F32)
        mrow = AR.alloc(cb * 128, F32)
        rrow = AR.alloc(cb * 128, F32)
        rep_m = AR.alloc(cb * 128, F32)
        rep_r = AR.alloc(cb * 128, F32)

        ps = [E(nc.psum_tensor(f"ps{i}", [128, 512], F32)) for i in range(8)]
        ps_sx, ps_sx2 = ps[0][0:1, :], ps[1][0:1, :]

        t_xbf = [Trk(), Trk()]
        t_x2b = [Trk(), Trk()]
        t_xf = [Trk(), Trk()]
        t_ln = [Trk(), Trk()]
        t_srow, t_mrow, t_rrow = Trk(), Trk(), Trk()
        t_rep = Trk()
        t_pssum = Trk()
        t_psrep = Trk()

        ntb = cb * 128 // 512
        for b in range(n_batch):
            ch0 = b * cb
            for tb in range(ntb):
                cA = ch0 + tb * 4
                for k in range(NK):
                    bi = k % 2
                    p.dma("gpsimd", xbf[bi].rearrange("p (c t) -> p c t", c=4),
                          xT_in[cA:cA + 4, k].rearrange("c p t -> p c t"),
                          f"XB{bi}", writes=[t_xbf[bi]])
                    p.step("scalar", lambda E, _i=bi: E.activation(
                        x2b[_i], xbf[_i], AF.Square), sem="AC",
                        reads=[t_xbf[bi]], writes=[t_x2b[bi]])
                    st_, sp_ = (k == 0), (k == NK - 1)
                    p.step("tensor", [
                        lambda E, _i=bi, _s=st_, _p=sp_: E.matmul(
                            ps_sx, ones_col_b, xbf[_i], start=_s, stop=_p),
                        lambda E, _i=bi, _s=st_, _p=sp_: E.matmul(
                            ps_sx2, ones_col_b, x2b[_i], start=_s, stop=_p),
                    ], sem="PE", reads=[t_xbf[bi], t_x2b[bi], t_cst],
                        writes=[t_pssum])
                p.step("scalar", [
                    lambda E, _t=tb: E.copy(srow[0:1, _t * 512:(_t + 1) * 512], ps_sx),
                    lambda E, _t=tb: E.copy(
                        srow[0:1, cb * 128 + _t * 512:cb * 128 + (_t + 1) * 512],
                        ps_sx2),
                ], sem="AC", rw=[t_pssum], writes=[t_srow])
            NT = cb * 128
            p.step("vector", lambda E: E.tensor_scalar(
                mrow[0:1, :], srow[0:1, 0:NT], 1.0 / D, None, op0=ALU.mult),
                sem="DV", reads=[t_srow], writes=[t_mrow])
            p.step("vector", lambda E: E.tensor_scalar(
                rrow[0:1, :], srow[0:1, NT:2 * NT], 1.0 / D, None, op0=ALU.mult),
                sem="DV", reads=[t_srow], writes=[t_rrow])
            p.step("scalar", lambda E: E.activation(
                srow[0:1, 0:NT], mrow[0:1, :], AF.Square), sem="AC",
                reads=[t_mrow], writes=[t_srow])
            p.step("vector", lambda E: E.tensor_tensor(
                rrow[0:1, :], rrow[0:1, :], srow[0:1, 0:NT], op=ALU.subtract),
                sem="DV", reads=[t_srow], rw=[t_rrow])
            p.step("scalar", lambda E: E.activation(
                rrow[0:1, :], rrow[0:1, :], AF.Ln, bias=eps_col[0:1, :]),
                sem="AC", rw=[t_rrow])
            p.step("scalar", lambda E: E.activation(
                rrow[0:1, :], rrow[0:1, :], AF.Exp, scale=-0.5),
                sem="AC", rw=[t_rrow])
            for piece in range(ntb):
                sl = slice(piece * 512, (piece + 1) * 512)
                p.step("tensor", [
                    lambda E, _s=sl: E.matmul(ps[2][:], ones_row_f[0:1, :],
                                              mrow[0:1, _s], start=True, stop=True),
                    lambda E, _s=sl: E.matmul(ps[3][:], ones_row_f[0:1, :],
                                              rrow[0:1, _s], start=True, stop=True),
                ], sem="PE", reads=[t_mrow, t_rrow, t_cst], writes=[t_psrep])
                p.step("scalar", [
                    lambda E, _s=sl: E.copy(rep_m[:, _s], ps[2][:]),
                    lambda E, _s=sl: E.copy(rep_r[:, _s], ps[3][:]),
                ], sem="AC", rw=[t_psrep], writes=[t_rep])
            for k in range(NK):
                bi = k % 2
                p.dma("sync", xf32[bi].rearrange("p (c t) -> p c t", c=cb),
                      xT_in[ch0:ch0 + cb, k].rearrange("c p t -> p c t"),
                      f"XF{bi}", writes=[t_xf[bi]])
                p.step("vector", lambda E, _i=bi: E.tensor_tensor(
                    xf32[_i], xf32[_i], rep_m, op=ALU.subtract),
                    sem="DV", reads=[t_rep], rw=[t_xf[bi]])
                p.step("vector", lambda E, _i=bi: E.tensor_tensor(
                    lnout[_i], xf32[_i], rep_r, op=ALU.mult),
                    sem="DV", reads=[t_rep, t_xf[bi]], writes=[t_ln[bi]])
                p.dma("sync",
                      ln1T[ch0:ch0 + cb, k].rearrange("c p t -> p c t"),
                      lnout[bi].rearrange("p (c t) -> p c t", c=cb),
                      f"LS{bi}", reads=[t_ln[bi]])
        AR.pop()

        if "ppre" in phases or "p1" in phases:
            p.full_barrier()
            AR.push()

            from types import SimpleNamespace

            S = [[AR.alloc(128, F32) for _ in range(2)] for _ in range(NKV)]
            S_bf = [[AR.alloc(128, BF16) for _ in range(2)] for _ in range(NKV)]
            t_S = [[Trk(), Trk()] for _ in range(NKV)]
            t_Sbf = [[Trk(), Trk()] for _ in range(NKV)]
            Vring = [AR.alloc(128, BF16) for _ in range(2)]
            t_V = [Trk(), Trk()]

            def mk_kv_ns():
                n = SimpleNamespace()
                for nm, cols, dt in [("ek", 128, F32), ("absk", 128, F32),
                                     ("gtmp", 128, F32), ("kmin", 128, F32),
                                     ("gs", 128, F32), ("em", 128, F32),
                                     ("ep", 128, F32), ("ktf", 128, F32),
                                     ("ktil", 128, BF16), ("sigk", 1, F32),
                                     ("recipk", 1, F32), ("gam", 1, F32),
                                     ("stmp", 128, F32), ("krb", 128, BF16),
                                     ("t1k", 128, F32), ("t2k", 128, F32),
                                     ("ktT", 128, BF16)]:
                    setattr(n, nm, AR.alloc(cols, dt))
                    setattr(n, "t_" + nm, Trk())
                return n

            KV = [mk_kv_ns(), mk_kv_ns()]
            FQ = [ps[0][:, :], ps[6][:, :]]
            t_FQ = [Trk(), Trk()]
            FKV = [ps[1][:, 0:256], ps[7][:, 0:256]]
            FG = [ps[1][:, 256:384], ps[7][:, 256:384]]
            FSp = [ps[1][:, 384:512], ps[7][:, 384:512]]
            t_FK = [Trk(), Trk()]
            ps_gam = ps[2][:, 508:509]
            t_ps2 = Trk()
            t_ps5 = Trk()
            t_sc34 = [Trk(), Trk()]
            tq_ctr = [0]

            def emit_kv_math(B, pr, do_gla, v_slot):
                if do_gla:
                    p.step("scalar", lambda E, _B=B, _P=FKV[pr]: E.activation(
                        _B.ek, _P[:, 0:128], AF.Exp, accum_out=_B.sigk),
                        sem="AC", rw=[t_FK[pr]], writes=[B.t_ek, B.t_sigk])
                    p.step("vector", lambda E, _B=B: E.reciprocal(
                        _B.recipk, _B.sigk), sem="DV",
                        reads=[B.t_sigk], writes=[B.t_recipk])
                    p.step("scalar", lambda E, _B=B, _P=FKV[pr]: E.activation(
                        _B.absk, _P[:, 0:128], AF.Abs), sem="AC",
                        rw=[t_FK[pr]], writes=[B.t_absk])
                    p.step("scalar", lambda E, _B=B: E.activation(
                        _B.gtmp, _B.absk, AF.Exp, scale=-1.0), sem="AC",
                        reads=[B.t_absk], writes=[B.t_gtmp])
                    p.step("scalar", lambda E, _B=B: E.activation(
                        _B.gtmp, _B.gtmp, AF.Ln, bias=ones_col_f), sem="AC",
                        rw=[B.t_gtmp])
                    p.step("vector", lambda E, _B=B, _P=FKV[pr]: E.tensor_scalar(
                        _B.kmin, _P[:, 0:128], 0.0, None, op0=ALU.min),
                        sem="DV", rw=[t_FK[pr]], writes=[B.t_kmin])
                    p.step("vector", lambda E, _B=B: E.tensor_tensor(
                        _B.gs, _B.kmin, _B.gtmp, op=ALU.subtract),
                        sem="DV", reads=[B.t_kmin, B.t_gtmp], writes=[B.t_gs])
                    p.step("tensor", lambda E, _B=B, _G=FG[pr]: E.matmul(
                        _G, tri16, _B.gs, start=True, stop=True),
                        sem="PE", reads=[B.t_gs, t_cst], rw=[t_FK[pr]])
                    p.step("scalar", [
                        lambda E, _B=B, _G=FG[pr]: E.activation(
                            _B.em, _G, AF.Exp, scale=-1.0),
                        lambda E, _B=B, _G=FG[pr]: E.activation(
                            _B.ep, _G, AF.Exp),
                    ], sem="AC", rw=[t_FK[pr]], writes=[B.t_em, B.t_ep])
                    p.step("vector", lambda E, _B=B: E.tensor_tensor(
                        _B.ktf, _B.ek, _B.em, op=ALU.mult),
                        sem="DV", reads=[B.t_ek, B.t_em], writes=[B.t_ktf])
                    p.step("vector", lambda E, _B=B: E.tensor_scalar(
                        _B.ktil, _B.ktf, _B.recipk, None, op0=ALU.mult),
                        sem="DV", reads=[B.t_ktf, B.t_recipk],
                        writes=[B.t_ktil])
                    p.step("tensor", lambda E, _B=B: E.matmul(
                        ps_gam, _B.ep, onehot, start=True, stop=True),
                        sem="PE", reads=[B.t_ep, t_cst], rw=[t_ps2])
                    p.step("scalar", lambda E, _B=B: E.copy(_B.gam, ps_gam),
                           sem="AC", rw=[t_ps2], writes=[B.t_gam])
                p.step("scalar", lambda E, _s=v_slot, _P=FKV[pr]: E.copy(
                    Vring[_s], _P[:, 128:256]),
                    sem="AC", rw=[t_FK[pr]], writes=[t_V[v_slot]])

            def emit_state_update(B, pr, g, par, v_slot):
                p.step("tensor", lambda E, _B=B, _s=v_slot, _Sp=FSp[pr]:
                       E.matmul(_Sp, _B.ktil, Vring[_s], start=True,
                                stop=True),
                       sem="PE", reads=[B.t_ktil, t_V[v_slot]], rw=[t_FK[pr]])
                p.step("vector", lambda E, _B=B, _g=g, _p=par, _Sp=FSp[pr]:
                       E.tensor_tensor(_B.stmp, S[_g][1 - _p], _Sp,
                                       op=ALU.add),
                       sem="DV", reads=[t_S[g][1 - par]], rw=[t_FK[pr]],
                       writes=[B.t_stmp])
                p.step("vector", lambda E, _B=B, _g=g, _p=par: E.tensor_scalar(
                    S[_g][_p], _B.stmp, _B.gam, None, op0=ALU.mult),
                    sem="DV", reads=[B.t_stmp, B.t_gam], writes=[t_S[g][par]])

            def transpose_to(dst_ap, src_ap, t_src, t_dst, sem, qn="sync"):
                qn = "sync"
                p.step(qn, lambda E, _d=dst_ap, _s=src_ap: E.dma_start(
                    out=_d, in_=_s, transpose=True), sem=sem, inc=16,
                    reads=[t_src], writes=[t_dst], dma=True)

            def pe_transpose_to(dst_ap, src_ap, t_src, t_dst, slot_ap,
                                t_bank):
                p.step("tensor", lambda E, _a=slot_ap, _s=src_ap: E.transpose(
                    _a, _s, ident_b), sem="PE",
                    reads=[t_src, t_cst], rw=[t_bank])
                p.step("scalar", lambda E, _a=slot_ap, _d=dst_ap: E.copy(
                    _d, _a), sem="AC", rw=[t_bank], writes=[t_dst])

        if "ppre" in phases:
            AR.push()
            wkv = AR.alloc(NKV * NK * 256, BF16)
            wkv_v = wkv.rearrange("p (g k c) -> p g k c", g=NKV, k=NK)
            t_wkv = Trk()
            for g in range(NKV):
                p.dma("gpsimd", wkv_v[:, g],
                      wqkv_in[g, :, :, 512:768].rearrange("k p c -> p k c"),
                      "WKV", writes=[t_wkv])
            lnst = [AR.alloc(NK * 128, BF16) for _ in range(2)]
            t_lnst = [Trk(), Trk()]
            for g in range(NKV):
                p.step("vector", lambda E, _g=g: E.memset(S[_g][0], 0.0),
                       sem="DV", writes=[t_S[g][0]])
            it = 0
            for c in range(n_pre_ch):
                sl = c % 2
                p.dma("sync", lnst[sl].rearrange("p (k t) -> p k t", k=NK),
                      ln1T[c].rearrange("k p t -> p k t"), f"LN{sl}",
                      writes=[t_lnst[sl]])
                for g in range(NKV):
                    pr = it % 2
                    it += 1
                    par = (c + 1) % 2
                    B = KV[pr]
                    fns = []
                    for k in range(NK):
                        fns.append(lambda E, _g=g, _k=k, _s=sl, _P=FKV[pr]:
                                   E.matmul(
                            _P, lnst[_s].rearrange(
                                "p (k t) -> p k t", k=NK)[:, _k, :],
                            wkv_v[:, _g, _k, :],
                            start=(_k == 0), stop=(_k == NK - 1)))
                    p.step("tensor", fns, sem="PE",
                           reads=[t_lnst[sl], t_wkv], rw=[t_FK[pr]])
                    emit_kv_math(B, pr, do_gla=True, v_slot=0)
                    emit_state_update(B, pr, g, par, 0)
            if "Sin" in debug_taps:
                par = n_pre_ch % 2
                sdbg = AR.alloc(128, F32)
                t_sdbg = Trk()
                for g in range(NKV):
                    p.step("scalar", lambda E, _g=g, _p=par: E.copy(
                        sdbg, S[_g][_p]), sem="AC",
                        reads=[t_S[g][par]], writes=[t_sdbg])
                    p.dma("sync", sin_out[g], sdbg, "SO", reads=[t_sdbg])
            AR.pop()

        if "p1" in phases:
            p.full_barrier()
            AR.push()
            wqb = [AR.alloc(NK * 768, BF16) for _ in range(2)]
            t_wq = [Trk(), Trk()]
            tblq = AR.alloc(n_sw * 2 * 128, BF16)
            tblq_v = tblq.rearrange("p (s i c) -> p s i c", s=n_sw, i=2)
            tblk = AR.alloc(n_sw * 2 * 128, BF16)
            tblk_v = tblk.rearrange("p (s i c) -> p s i c", s=n_sw, i=2)
            maskR = AR.alloc(768, BF16)
            maskA = AR.alloc(768, BF16)
            trim = AR.alloc(512, BF16)
            t_tbl = Trk()
            p.dma("gpsimd", tblq_v, ropeq_in.rearrange("s i p c -> p s i c"),
                  "TBL", writes=[t_tbl])
            p.dma("gpsimd", tblk_v, ropek_in.rearrange("s i p c -> p s i c"),
                  "TB2", writes=[t_tbl])
            p.dma("gpsimd", maskR, maskR_in, "TB3", writes=[t_tbl])
            p.dma("gpsimd", maskA, maskA_in, "TB4", writes=[t_tbl])
            p.dma("gpsimd", trim, trimask_in, "TB5", writes=[t_tbl])

            lnp = [AR.alloc(NK * 128, BF16) for _ in range(2)]
            t_lnp = [Trk(), Trk()]
            krT = [AR.alloc(128, BF16) for _ in range(2)]
            t_krT = [Trk(), Trk()]

            def mk_q_ns():
                n = SimpleNamespace()
                for nm, cols, dt in [("eq", 512, F32), ("sigq", 4, F32),
                                     ("recipq", 4, F32), ("qhu", 512, F32),
                                     ("qhat", 512, BF16), ("qt1", 512, F32),
                                     ("qt2", 512, F32), ("qrb", 512, BF16),
                                     ("qhT", 512, BF16), ("qrT", 512, BF16)]:
                    setattr(n, nm, AR.alloc(cols, dt))
                    setattr(n, "t_" + nm, Trk())
                return n

            QN = [mk_q_ns(), mk_q_ns()]
            atm = AR.alloc(512, BF16)
            pu = AR.alloc(768, F32)
            pm = AR.alloc(768, F32)
            ssum4 = AR.alloc(4, F32)
            rswa = AR.alloc(4, F32)
            pb = AR.alloc(4 * 256, BF16)
            ptA = AR.alloc(512, BF16)
            ptB = AR.alloc(512, BF16)
            oT_sb = [AR.alloc(512, BF16) for _ in range(2)]
            t_atm, t_pu, t_pm, t_ssum, t_rswa, t_pb = (
                Trk(), Trk(), Trk(), Trk(), Trk(), Trk())
            t_ptA, t_ptB = Trk(), Trk()
            t_oT = [Trk(), Trk()]

            def sc_ap(h):
                bank = ps[3] if h < 2 else ps[4]
                off = (h % 2) * 256
                return bank[:, off:off + 256], t_sc34[0 if h < 2 else 1]

            FQ_b = [ps[0].ap().bitcast(BF16), ps[6].ap().bitcast(BF16)]
            FG_b = [ps[1].ap().bitcast(BF16), ps[7].ap().bitcast(BF16)]
            SC_b = [ps[3].ap().bitcast(BF16), ps[4].ap().bitcast(BF16)]

            p.step("vector", lambda E: E.memset(pb, 0.0), sem="DV",
                   writes=[t_pb])
            it = 0
            for g in range(p1_groups):
                wsl = g % 2
                p.dma("gpsimd", wqb[wsl].rearrange("p (k c) -> p k c", k=NK),
                      wqkv_in[g].rearrange("k p c -> p k c"), f"WQ{wsl}",
                      writes=[t_wq[wsl]])
                par0 = n_pre_ch % 2
                p.step("scalar", lambda E, _g=g, _p=par0: E.copy(
                    S_bf[_g][_p], S[_g][_p]), sem="AC",
                    reads=[t_S[g][par0]], writes=[t_Sbf[g][par0]])
                for c in range(n_pre_ch - 1, n_ch):
                    pr = it % 2
                    it += 1
                    sl = c % 2
                    ti = c - (n_pre_ch - 1)
                    j = c - n_pre_ch
                    is_halo = (c == n_pre_ch - 1)
                    B = KV[pr]
                    Q = QN[pr]
                    wv_ = wqb[wsl].rearrange("p (k c) -> p k c", k=NK)
                    p.dma("sync", lnp[sl].rearrange("p (k t) -> p k t", k=NK),
                          ln1T[c].rearrange("k p t -> p k t"), f"LP{sl}",
                          writes=[t_lnp[sl]])
                    fns = []
                    for k in range(NK):
                        fns.append(lambda E, _k=k, _s=sl, _w=wv_, _P=FKV[pr]:
                                   E.matmul(
                            _P, lnp[_s].rearrange(
                                "p (k t) -> p k t", k=NK)[:, _k, :],
                            _w[:, _k, 512:768],
                            start=(_k == 0), stop=(_k == NK - 1)))
                    p.step("tensor", fns, sem="PE",
                           reads=[t_lnp[sl], t_wq[wsl]], rw=[t_FK[pr]])
                    if not is_halo:
                        fns = []
                        for k in range(NK):
                            fns.append(lambda E, _k=k, _s=sl, _w=wv_,
                                       _P=FQ[pr]: E.matmul(
                                _P, lnp[_s].rearrange(
                                    "p (k t) -> p k t", k=NK)[:, _k, :],
                                _w[:, _k, 0:512],
                                start=(_k == 0), stop=(_k == NK - 1)))
                        p.step("tensor", fns, sem="PE",
                               reads=[t_lnp[sl], t_wq[wsl]], rw=[t_FQ[pr]])
                    emit_kv_math(B, pr, do_gla=(not is_halo), v_slot=sl)
                    if not is_halo:
                        pe_transpose_to(B.ktT, B.ktil, B.t_ktil, B.t_ktT,
                                        FG_b[pr][:, 512:640], t_FK[pr])
                    p.step("vector", lambda E, _B=B, _t=ti, _P=FKV[pr]:
                           E.tensor_tensor(
                        _B.t1k, _P[:, 0:128], tblk_v[:, _t, 0, :],
                        op=ALU.mult),
                        sem="DV", reads=[t_tbl], rw=[t_FK[pr]],
                        writes=[B.t_t1k])
                    p.step("vector", [
                        lambda E, _B=B, _t=ti, _P=FKV[pr]: E.tensor_tensor(
                            _B.t2k[:, 0:64], _P[:, 64:128],
                            tblk_v[:, _t, 1, 0:64], op=ALU.mult),
                        lambda E, _B=B, _t=ti, _P=FKV[pr]: E.tensor_tensor(
                            _B.t2k[:, 64:128], _P[:, 0:64],
                            tblk_v[:, _t, 1, 64:128], op=ALU.mult),
                    ], sem="DV", reads=[t_tbl], rw=[t_FK[pr]],
                        writes=[B.t_t2k])
                    p.step("vector", lambda E, _B=B: E.tensor_tensor(
                        _B.krb, _B.t1k, _B.t2k, op=ALU.add),
                        sem="DV", reads=[B.t_t1k, B.t_t2k], writes=[B.t_krb])
                    transpose_to(krT[sl], B.krb, B.t_krb, t_krT[sl],
                                 f"TR{sl}", ("sync", "scalar")[sl])
                    if is_halo:
                        continue
                    for h in range(GRP):
                        p.step("scalar", lambda E, _Q=Q, _h=h, _P=FQ[pr]:
                               E.activation(
                            _Q.eq[:, _h * 128:(_h + 1) * 128],
                            _P[:, _h * 128:(_h + 1) * 128], AF.Exp,
                            accum_out=_Q.sigq[:, _h:_h + 1]), sem="AC",
                            rw=[t_FQ[pr]], writes=[Q.t_eq, Q.t_sigq])
                    p.step("vector", lambda E, _Q=Q: E.reciprocal(
                        _Q.recipq, _Q.sigq), sem="DV",
                        reads=[Q.t_sigq], writes=[Q.t_recipq])
                    p.step("vector", lambda E, _Q=Q, _B=B: E.tensor_tensor(
                        _Q.qhu.rearrange("p (h c) -> p h c", h=GRP),
                        _Q.eq.rearrange("p (h c) -> p h c", h=GRP),
                        _B.ep.unsqueeze(1).broadcast_to([128, GRP, 128]),
                        op=ALU.mult),
                        sem="DV", reads=[Q.t_eq, B.t_ep], writes=[Q.t_qhu])
                    for h in range(GRP):
                        p.step("vector", lambda E, _Q=Q, _h=h: E.tensor_scalar(
                            _Q.qhat[:, _h * 128:(_h + 1) * 128],
                            _Q.qhu[:, _h * 128:(_h + 1) * 128],
                            _Q.recipq[:, _h:_h + 1], None, op0=ALU.mult),
                            sem="DV", reads=[Q.t_qhu, Q.t_recipq],
                            writes=[Q.t_qhat])
                    p.step("vector", lambda E, _Q=Q, _t=ti, _P=FQ[pr]:
                           E.tensor_tensor(
                        _Q.qt1.rearrange("p (h c) -> p h c", h=GRP),
                        _P.rearrange("p (h c) -> p h c", h=GRP),
                        tblq_v[:, _t, 0, :].unsqueeze(1).broadcast_to(
                            [128, GRP, 128]), op=ALU.mult),
                        sem="DV", reads=[t_tbl], rw=[t_FQ[pr]],
                        writes=[Q.t_qt1])
                    p.step("vector", [
                        lambda E, _Q=Q, _t=ti, _P=FQ[pr]: E.tensor_tensor(
                            _Q.qt2.rearrange("p (h a c) -> p h a c", h=GRP,
                                             a=2)[:, :, 0, :],
                            _P.rearrange("p (h a c) -> p h a c", h=GRP,
                                         a=2)[:, :, 1, :],
                            tblq_v[:, _t, 1, 0:64].unsqueeze(1).broadcast_to(
                                [128, GRP, 64]), op=ALU.mult),
                        lambda E, _Q=Q, _t=ti, _P=FQ[pr]: E.tensor_tensor(
                            _Q.qt2.rearrange("p (h a c) -> p h a c", h=GRP,
                                             a=2)[:, :, 1, :],
                            _P.rearrange("p (h a c) -> p h a c", h=GRP,
                                         a=2)[:, :, 0, :],
                            tblq_v[:, _t, 1, 64:128].unsqueeze(1).broadcast_to(
                                [128, GRP, 64]), op=ALU.mult),
                    ], sem="DV", reads=[t_tbl], rw=[t_FQ[pr]],
                        writes=[Q.t_qt2])
                    p.step("vector", lambda E, _Q=Q: E.tensor_tensor(
                        _Q.qrb, _Q.qt1, _Q.qt2, op=ALU.add),
                        sem="DV", reads=[Q.t_qt1, Q.t_qt2], writes=[Q.t_qrb])
                    for h in range(GRP):
                        pe_transpose_to(Q.qhT[:, h * 128:(h + 1) * 128],
                                        Q.qhat[:, h * 128:(h + 1) * 128],
                                        Q.t_qhat, Q.t_qhT,
                                        FQ_b[pr][:, h * 128:(h + 1) * 128],
                                        t_FQ[pr])
                        pe_transpose_to(Q.qrT[:, h * 128:(h + 1) * 128],
                                        Q.qrb[:, h * 128:(h + 1) * 128],
                                        Q.t_qrb, Q.t_qrT,
                                        FQ_b[pr][:, 512 + h * 128:
                                                  512 + (h + 1) * 128],
                                        t_FQ[pr])
                    p.step("tensor", lambda E, _B=B, _Q=Q: E.matmul(
                        ps[2][:, 0:512], _B.ktT, _Q.qhT, start=True,
                        stop=True), sem="PE",
                        reads=[B.t_ktT, Q.t_qhT], rw=[t_ps2])
                    p.step("vector", lambda E: E.tensor_tensor(
                        atm, ps[2][:, 0:512], trim, op=ALU.mult), sem="DV",
                        reads=[t_tbl], rw=[t_ps2], writes=[t_atm])
                    for h in range(GRP):
                        ap_sc, t_sc = sc_ap(h)
                        p.step("tensor", [
                            lambda E, _Q=Q, _h=h, _a=ap_sc, _s=sl: E.matmul(
                                _a[:, 64:128],
                                _Q.qrT[:, _h * 128:(_h + 1) * 128],
                                krT[1 - _s][:, 64:128], start=True,
                                stop=True),
                            lambda E, _Q=Q, _h=h, _a=ap_sc, _s=sl: E.matmul(
                                _a[:, 128:256],
                                _Q.qrT[:, _h * 128:(_h + 1) * 128],
                                krT[_s], start=True, stop=True),
                        ], sem="PE",
                            reads=[Q.t_qrT, t_krT[sl], t_krT[1 - sl]],
                            rw=[t_sc])
                    for h in range(GRP):
                        ap_sc, t_sc = sc_ap(h)
                        p.step("scalar", lambda E, _h=h, _a=ap_sc:
                               E.activation(
                            pu[:, _h * 192:(_h + 1) * 192], _a[:, 64:256],
                            AF.Exp), sem="AC", rw=[t_sc], writes=[t_pu])
                    mk = maskA if j == 0 else maskR
                    p.step("vector", lambda E, _m=mk: E.tensor_tensor(
                        pm, pu, _m, op=ALU.mult), sem="DV",
                        reads=[t_pu, t_tbl], writes=[t_pm])
                    p.step("vector", lambda E: E.tensor_reduce(
                        ssum4, pm.rearrange("p (h c) -> p h c", h=GRP),
                        axis=AX.X, op=ALU.add), sem="DV",
                        reads=[t_pm], writes=[t_ssum])
                    p.step("vector", lambda E: E.reciprocal(rswa, ssum4),
                           sem="DV", reads=[t_ssum], writes=[t_rswa])
                    for h in range(GRP):
                        p.step("vector", lambda E, _h=h: E.tensor_scalar(
                            pb[:, _h * 256 + 64:(_h + 1) * 256],
                            pm[:, _h * 192:(_h + 1) * 192],
                            rswa[:, _h:_h + 1], 0.5, op0=ALU.mult,
                            op1=ALU.mult),
                            sem="DV", reads=[t_pm, t_rswa], writes=[t_pb])
                    for h in range(GRP):
                        bsel = 0 if h < 2 else 1
                        off = (h % 2) * 512
                        pe_transpose_to(ptA[:, h * 128:(h + 1) * 128],
                                        pb[:, h * 256:h * 256 + 128],
                                        t_pb, t_ptA,
                                        SC_b[bsel][:, off:off + 128],
                                        t_sc34[bsel])
                        pe_transpose_to(ptB[:, h * 128:(h + 1) * 128],
                                        pb[:, h * 256 + 128:(h + 1) * 256],
                                        t_pb, t_ptB,
                                        SC_b[bsel][:, off + 128:off + 256],
                                        t_sc34[bsel])
                    par = c % 2
                    mmops = []
                    if o_parts in ("both", "gla"):
                        mmops.append((Vring[sl], atm, None, t_atm))
                        mmops.append((S_bf[g][par], Q.qhT, None, Q.t_qhT))
                    if o_parts in ("both", "swa"):
                        for h in range(GRP):
                            mmops.append((Vring[1 - sl][64:128, :],
                                          ptA[64:128,
                                              h * 128:(h + 1) * 128],
                                          (h * 128, (h + 1) * 128), t_ptA))
                            mmops.append((Vring[sl],
                                          ptB[:, h * 128:(h + 1) * 128],
                                          (h * 128, (h + 1) * 128), t_ptB))
                    ofns = []
                    for i2, (lh, rh, cols, _) in enumerate(mmops):
                        outap = (ps[5][:, :] if cols is None
                                 else ps[5][:, cols[0]:cols[1]])
                        ofns.append(lambda E, _l=lh, _r=rh, _o=outap,
                                    _s=(i2 == 0), _p=(i2 == len(mmops) - 1):
                                    E.matmul(_o, _l, _r, start=_s, stop=_p))
                    p.step("tensor", ofns, sem="PE",
                           reads=[t_V[sl], t_V[1 - sl], t_atm,
                                  t_Sbf[g][par], Q.t_qhT, t_ptA, t_ptB],
                           rw=[t_ps5])
                    p.step("scalar", lambda E, _s=sl: E.copy(
                        oT_sb[_s], ps[5][:, :]), sem="AC",
                        rw=[t_ps5], writes=[t_oT[sl]])
                    p.dma("sync", oT_dram[j, g], oT_sb[sl], f"OS{sl}",
                          reads=[t_oT[sl]])
                    emit_state_update(B, pr, g, (c + 1) % 2, sl)
                    p.step("scalar", lambda E, _g=g, _p=(c + 1) % 2: E.copy(
                        S_bf[_g][_p], S[_g][_p]), sem="AC",
                        reads=[t_S[g][(c + 1) % 2]],
                        writes=[t_Sbf[g][(c + 1) % 2]])
            AR.pop()

        if "ppre" in phases or "p1" in phases:
            AR.pop()

        ntok = n_own_ch * 128
        nts = ntok // 512

        if "p2" in phases:
            p.full_barrier()
            AR.push()
            oTv = AR.alloc(n_own_ch * NKV * 512, BF16).rearrange(
                "p (j g c) -> p j g c", j=n_own_ch, g=NKV)
            t_oTr = Trk()
            for j_ in range(n_own_ch):
                p.dma("gpsimd", oTv[:, j_], oT_dram[j_].rearrange(
                    "g p c -> p g c"), "OTL", writes=[t_oTr])
            wot = [AR.alloc(NK * 128, BF16) for _ in range(2)]
            t_wot = [Trk(), Trk()]
            xts = [AR.alloc(512, F32) for _ in range(2)]
            t_xts = [Trk(), Trk()]
            x1b = [AR.alloc(512, BF16) for _ in range(2)]
            t_x1b = [Trk(), Trk()]
            t_ps0 = Trk()
            for jc in range(NK):
                wsl = jc % 2
                p.dma("gpsimd", wot[wsl].rearrange("p (h c) -> p h c", h=NK),
                      wo_in[jc].rearrange("h p c -> p h c"), f"WO{wsl}",
                      writes=[t_wot[wsl]])
                for ts in range(nts):
                    xsl = ts % 2
                    j0 = ts * 4
                    p.dma("sync", xts[xsl].rearrange("p (c t) -> p c t", c=4),
                          xT_in[n_pre_ch + j0:n_pre_ch + j0 + 4, jc].rearrange(
                              "c p t -> p c t"), f"XS{xsl}",
                          writes=[t_xts[xsl]])
                    fns = []
                    for hd in range(NK):
                        gg, hh = hd // GRP, hd % GRP
                        fns.append(lambda E, _h=hd, _g=gg, _hh=hh, _j=j0,
                                   _w=wsl: E.matmul(
                            ps[0][:, :],
                            wot[_w].rearrange("p (h c) -> p h c", h=NK)[:, _h, :],
                            oTv[:, _j:_j + 4, _g,
                                _hh * 128:(_hh + 1) * 128],
                            start=(_h == 0), stop=(_h == NK - 1)))
                    p.step("tensor", fns, sem="PE",
                           reads=[t_oTr, t_wot[wsl]], writes=[t_ps0])
                    p.step("vector", lambda E, _x=xsl, _b=xsl: E.tensor_tensor(
                        x1b[_b], ps[0][:, :], xts[_x], op=ALU.add),
                        sem="DV", reads=[t_xts[xsl]], rw=[t_ps0],
                        writes=[t_x1b[xsl]])
                    p.dma("sync", x1T_dram[jc, :, ts * 512:(ts + 1) * 512],
                          x1b[xsl], f"XW{xsl}", reads=[t_x1b[xsl]])
            AR.pop()

        if "p3" in phases:
            p.full_barrier()
            AR.push()
            ln2T = AR.alloc(NK * ntok, BF16).rearrange(
                "p (k t) -> p k t", k=NK)
            t_ln2 = Trk()
            AR.push()
            x1s = [AR.alloc(ntok, BF16) for _ in range(2)]
            t_x1s = [Trk(), Trk()]
            x2s = AR.alloc(ntok, BF16)
            t_x2s = Trk()
            srow2 = AR.alloc(2 * ntok, F32)
            m2row = AR.alloc(ntok, F32)
            r2row = AR.alloc(ntok, F32)
            t_sr2, t_m2, t_r2 = Trk(), Trk(), Trk()
            rep2_m = AR.alloc(ntok, F32)
            rep2_r = AR.alloc(ntok, F32)
            t_rep2 = Trk()
            t_pst = [Trk() for _ in range(8)]
            for jc in range(NK):
                xsl = jc % 2
                p.dma("sync", x1s[xsl], x1T_dram[jc], f"X1{xsl}",
                      writes=[t_x1s[xsl]])
                p.step("scalar", lambda E, _x=xsl: E.activation(
                    x2s, x1s[_x], AF.Square), sem="AC",
                    reads=[t_x1s[xsl]], writes=[t_x2s])
                for ts in range(nts):
                    st_, sp_ = (jc == 0), (jc == NK - 1)
                    p.step("tensor", [
                        lambda E, _x=xsl, _t=ts, _s=st_, _p=sp_: E.matmul(
                            ps[_t][0:1, :], ones_col_b,
                            x1s[_x][:, _t * 512:(_t + 1) * 512],
                            start=_s, stop=_p),
                        lambda E, _t=ts, _s=st_, _p=sp_: E.matmul(
                            ps[4 + _t][0:1, :], ones_col_b,
                            x2s[:, _t * 512:(_t + 1) * 512],
                            start=_s, stop=_p),
                    ], sem="PE", reads=[t_x1s[xsl], t_x2s, t_cst],
                        writes=[t_pst[ts], t_pst[4 + ts]])
            for ts in range(nts):
                p.step("scalar", [
                    lambda E, _t=ts: E.copy(
                        srow2[0:1, _t * 512:(_t + 1) * 512], ps[_t][0:1, :]),
                    lambda E, _t=ts: E.copy(
                        srow2[0:1, ntok + _t * 512:ntok + (_t + 1) * 512],
                        ps[4 + _t][0:1, :]),
                ], sem="AC", rw=[t_pst[ts], t_pst[4 + ts]], writes=[t_sr2])
            p.step("vector", lambda E: E.tensor_scalar(
                m2row[0:1, :], srow2[0:1, 0:ntok], 1.0 / D, None, op0=ALU.mult),
                sem="DV", reads=[t_sr2], writes=[t_m2])
            p.step("vector", lambda E: E.tensor_scalar(
                r2row[0:1, :], srow2[0:1, ntok:2 * ntok], 1.0 / D, None,
                op0=ALU.mult), sem="DV", reads=[t_sr2], writes=[t_r2])
            p.step("scalar", lambda E: E.activation(
                srow2[0:1, 0:ntok], m2row[0:1, :], AF.Square), sem="AC",
                reads=[t_m2], writes=[t_sr2])
            p.step("vector", lambda E: E.tensor_tensor(
                r2row[0:1, :], r2row[0:1, :], srow2[0:1, 0:ntok],
                op=ALU.subtract), sem="DV", reads=[t_sr2], rw=[t_r2])
            p.step("scalar", lambda E: E.activation(
                r2row[0:1, :], r2row[0:1, :], AF.Ln, bias=eps_col[0:1, :]),
                sem="AC", rw=[t_r2])
            p.step("scalar", lambda E: E.activation(
                r2row[0:1, :], r2row[0:1, :], AF.Exp, scale=-0.5),
                sem="AC", rw=[t_r2])
            for piece in range(nts):
                sl2 = slice(piece * 512, (piece + 1) * 512)
                p.step("tensor", [
                    lambda E, _s=sl2: E.matmul(ps[0][:, :], ones_row_f[0:1, :],
                                               m2row[0:1, _s], start=True,
                                               stop=True),
                    lambda E, _s=sl2: E.matmul(ps[1][:, :], ones_row_f[0:1, :],
                                               r2row[0:1, _s], start=True,
                                               stop=True),
                ], sem="PE", reads=[t_m2, t_r2, t_cst],
                    writes=[t_pst[0], t_pst[1]])
                p.step("scalar", [
                    lambda E, _s=sl2: E.copy(rep2_m[:, _s], ps[0][:, :]),
                    lambda E, _s=sl2: E.copy(rep2_r[:, _s], ps[1][:, :]),
                ], sem="AC", rw=[t_pst[0], t_pst[1]], writes=[t_rep2])
            for jc in range(NK):
                xsl = jc % 2
                p.dma("sync", x1s[xsl], x1T_dram[jc], f"X1{xsl}",
                      writes=[t_x1s[xsl]])
                p.step("vector", lambda E, _x=xsl: E.tensor_tensor(
                    x1s[_x], x1s[_x], rep2_m, op=ALU.subtract),
                    sem="DV", reads=[t_rep2], rw=[t_x1s[xsl]])
                p.step("vector", lambda E, _x=xsl, _j=jc: E.tensor_tensor(
                    ln2T[:, _j, :], x1s[_x], rep2_r, op=ALU.mult),
                    sem="DV", reads=[t_rep2, t_x1s[xsl]], writes=[t_ln2])
                if ln2dbg is not None:
                    p.dma("sync", ln2dbg[jc], ln2T[:, jc, :], "L2D",
                          reads=[t_ln2])
            AR.pop()

        if "p4" in phases:
            p.full_barrier()
            wrt = AR.alloc(NK * 8, BF16).rearrange("p (k e) -> p k e", k=NK)
            att = AR.alloc(NK * 128, BF16).rearrange("p (k c) -> p k c", k=NK)
            bcol = AR.alloc(NK, F32)
            t_wrt, t_att, t_bcol = Trk(), Trk(), Trk()
            p.dma("gpsimd", wrt, wr_in.rearrange("k p e -> p k e"), "P4C",
                  writes=[t_wrt])
            p.dma("gpsimd", att, at_in.rearrange("k p c -> p k c"), "P4D",
                  writes=[t_att])
            p.dma("sync", bcol, b_in, "P4B", writes=[t_bcol])
            lgts = AR.alloc(8, F32)
            m1c = AR.alloc(1, F32)
            mk1 = AR.alloc(8, F32)
            tm8 = AR.alloc(8, F32)
            m2c = AR.alloc(1, F32)
            mk2 = AR.alloc(8, F32)
            w1c = AR.alloc(1, F32)
            w2c = AR.alloc(1, F32)
            wfld = AR.alloc(8, F32)
            hw = AR.alloc(128, BF16)
            hwT = AR.alloc(ntok, BF16)
            t_lg, t_m1c, t_mk1, t_tm8, t_m2c, t_mk2 = (
                Trk(), Trk(), Trk(), Trk(), Trk(), Trk())
            t_w1, t_w2, t_wf, t_hw, t_hwT = Trk(), Trk(), Trk(), Trk(), Trk()
            t_psr = Trk()
            t_psh = Trk()
            for tc in range(n_own_ch):
                tsl = slice(tc * 128, (tc + 1) * 128)
                fns = [lambda E, _k=k2, _t=tc: E.matmul(
                    ps[2][:, 0:8], ln2T[:, _k, _t * 128:(_t + 1) * 128],
                    wrt[:, _k, :], start=(_k == 0), stop=(_k == NK - 1))
                    for k2 in range(NK)]
                p.step("tensor", fns, sem="PE", reads=[t_ln2, t_wrt],
                       writes=[t_psr])
                p.step("vector", lambda E: E.tensor_copy(lgts, ps[2][:, 0:8]),
                       sem="DV", rw=[t_psr], writes=[t_lg])
                p.step("vector", lambda E: E.tensor_reduce(
                    m1c, lgts, axis=AX.X, op=ALU.max), sem="DV",
                    reads=[t_lg], writes=[t_m1c])
                p.step("vector", lambda E: E.tensor_tensor(
                    mk1, lgts, m1c.broadcast_to([128, 8]), op=ALU.is_ge),
                    sem="DV", reads=[t_lg, t_m1c], writes=[t_mk1])
                p.step("vector", lambda E: E.tensor_scalar(
                    tm8, mk1, -1e30, None, op0=ALU.mult), sem="DV",
                    reads=[t_mk1], writes=[t_tm8])
                p.step("vector", lambda E: E.tensor_tensor(
                    tm8, lgts, tm8, op=ALU.add), sem="DV",
                    reads=[t_lg], rw=[t_tm8])
                p.step("vector", lambda E: E.tensor_reduce(
                    m2c, tm8, axis=AX.X, op=ALU.max), sem="DV",
                    reads=[t_tm8], writes=[t_m2c])
                p.step("vector", lambda E: E.tensor_tensor(
                    mk2, tm8, m2c.broadcast_to([128, 8]), op=ALU.is_ge),
                    sem="DV", reads=[t_tm8, t_m2c], writes=[t_mk2])
                p.step("vector", lambda E: E.tensor_tensor(
                    w1c, m2c, m1c, op=ALU.subtract), sem="DV",
                    reads=[t_m1c, t_m2c], writes=[t_w1])
                p.step("scalar", lambda E: E.activation(
                    w1c, w1c, AF.Exp), sem="AC", rw=[t_w1])
                p.step("vector", lambda E: E.tensor_scalar(
                    w1c, w1c, 1.0, None, op0=ALU.add), sem="DV", rw=[t_w1])
                p.step("vector", lambda E: E.reciprocal(w1c, w1c),
                       sem="DV", rw=[t_w1])
                p.step("vector", lambda E: E.tensor_scalar(
                    w1c, w1c, 2.0, None, op0=ALU.mult), sem="DV", rw=[t_w1])
                p.step("vector", lambda E: E.tensor_scalar(
                    w2c, w1c, -1.0, 2.0, op0=ALU.mult, op1=ALU.add),
                    sem="DV", reads=[t_w1], writes=[t_w2])
                p.step("vector", lambda E: E.tensor_scalar(
                    wfld, mk1, w1c[:, 0:1], None, op0=ALU.mult), sem="DV",
                    reads=[t_mk1, t_w1], writes=[t_wf])
                p.step("vector", lambda E: E.tensor_scalar(
                    tm8, mk2, w2c[:, 0:1], None, op0=ALU.mult), sem="DV",
                    reads=[t_mk2, t_w2], rw=[t_tm8])
                p.step("vector", lambda E: E.tensor_tensor(
                    wfld, wfld, tm8, op=ALU.add), sem="DV",
                    reads=[t_tm8], rw=[t_wf])
                fns = [lambda E, _k=k2, _t=tc: E.matmul(
                    ps[3][:, 0:128], ln2T[:, _k, _t * 128:(_t + 1) * 128],
                    att[:, _k, :], start=(_k == 0), stop=(_k == NK - 1))
                    for k2 in range(NK)]
                p.step("tensor", fns, sem="PE", reads=[t_ln2, t_att],
                       writes=[t_psh])
                p.step("vector", lambda E: E.tensor_tensor(
                    hw.rearrange("p (e r) -> p e r", e=8),
                    ps[3][:, 0:128].rearrange("p (e r) -> p e r", e=8),
                    wfld.unsqueeze(2).broadcast_to([128, 8, 16]),
                    op=ALU.mult), sem="DV", reads=[t_wf], rw=[t_psh],
                    writes=[t_hw])
                p.step("scalar", lambda E, _t=tc: E.dma_start(
                    out=hwT[:, _t * 128:(_t + 1) * 128], in_=hw,
                    transpose=True), sem="THW", inc=16,
                    reads=[t_hw], writes=[t_hwT], dma=True)
            wbt = [AR.alloc(NK * 128, BF16) for _ in range(2)]
            bft = [AR.alloc(128, BF16) for _ in range(2)]
            t_wbt = [Trk(), Trk()]
            t_bft = [Trk(), Trk()]
            x1r = [AR.alloc(512, BF16) for _ in range(2)]
            t_x1r = [Trk(), Trk()]
            of32 = [AR.alloc(512, F32) for _ in range(2)]
            t_of = [Trk(), Trk()]
            og32 = [AR.alloc(512, F32) for _ in range(2)]
            t_og = [Trk(), Trk()]
            t_ps45 = Trk()
            t_ps67 = [Trk(), Trk()]
            for oc in range(NK):
                wsl = oc % 2
                p.dma("gpsimd", wbt[wsl].rearrange("p (k c) -> p k c", k=NK),
                      wbT_in[oc].rearrange("k p c -> p k c"), f"WB{wsl}",
                      writes=[t_wbt[wsl]])
                p.dma("gpsimd", bft[wsl], bf_in[oc], f"BF{wsl}",
                      writes=[t_bft[wsl]])
                for ts in range(nts):
                    xsl = ts % 2
                    fns = [lambda E, _k=k2, _t=ts, _w=wsl: E.matmul(
                        ps[4][:, :],
                        wbt[_w].rearrange("p (k c) -> p k c", k=NK)[:, _k, :],
                        ln2T[:, _k, _t * 512:(_t + 1) * 512],
                        start=(_k == 0), stop=False)
                        for k2 in range(NK)]
                    fns.append(lambda E, _t=ts, _w=wsl: E.matmul(
                        ps[4][:, :], bft[_w],
                        hwT[:, _t * 512:(_t + 1) * 512],
                        start=False, stop=True))
                    p.step("tensor", fns, sem="PE",
                           reads=[t_ln2, t_wbt[wsl], t_bft[wsl], t_hwT],
                           writes=[t_ps45])
                    p.dma("sync", x1r[xsl],
                          x1T_dram[oc, :, ts * 512:(ts + 1) * 512],
                          f"XR{xsl}", writes=[t_x1r[xsl]])
                    p.step("vector", lambda E, _o=oc, _x=xsl: E.tensor_scalar(
                        of32[_x], ps[4][:, :], bcol[:, _o:_o + 1], None,
                        op0=ALU.add), sem="DV", rw=[t_ps45],
                        reads=[t_bcol], writes=[t_of[xsl]])
                    p.step("vector", lambda E, _x=xsl: E.tensor_tensor(
                        og32[_x], of32[_x], x1r[_x], op=ALU.add),
                        sem="DV", reads=[t_of[xsl], t_x1r[xsl]],
                        writes=[t_og[xsl]])
                    for q4 in range(4):
                        pb_ = 6 + (q4 % 2)
                        p.step("tensor", lambda E, _q=q4, _x=xsl, _b=pb_:
                               E.transpose(ps[_b][:, 0:128],
                                           og32[_x][:, _q * 128:(_q + 1) * 128],
                                           ident_f), sem="PE",
                               reads=[t_og[xsl], t_cst],
                               rw=[t_ps67[q4 % 2]])
                        p.step("scalar", lambda E, _b=pb_, _x=xsl, _q=q4:
                               E.copy(of32[_x][:, _q * 128:(_q + 1) * 128],
                                      ps[_b][:, 0:128]), sem="AC",
                               rw=[t_ps67[q4 % 2]], writes=[t_of[xsl]])
                    p.dma("sync",
                          out_hbm[ts * 512:(ts + 1) * 512,
                                  oc * 128:(oc + 1) * 128].rearrange(
                              "(a t) f -> t a f", a=4),
                          of32[xsl].rearrange("p (a f) -> p a f", a=4),
                          f"OW{xsl}", reads=[t_of[xsl]])
            AR.pop()

        p.full_barrier()

        sems = {}
        for name in sorted(p.ctr):
            sems[name] = E(nc.semaphore(name))
        block = E(nc.Block())

        def run_stream(eng_name):
            def body(Eng):
                Eng._sems = sems
                for fn in p.streams[eng_name]:
                    fn(Eng)
            return body

        block.sync(run_stream("sync"))
        block.gpsimd(run_stream("gpsimd"))
        block.tensor(run_stream("tensor"))
        block.vector(run_stream("vector"))
        block.scalar(run_stream("scalar"))

    return nc



def make_consts():
    c = np.zeros((8, 128, 128), np.float32)
    c[0] = np.eye(128, dtype=np.float32)
    c[1] = (np.tril(np.ones((128, 128), np.float32)) / GATE_NORM).T
    c[2][127, 0] = 1.0
    c[3][:] = 1.0
    c[4][:] = EPS
    return c


def make_xT_tiled(x_core):
    tok = x_core.shape[0]
    n_ch = tok // 128
    t = x_core.reshape(n_ch, 128, NK, 128)
    return np.ascontiguousarray(t.transpose(0, 2, 3, 1))


def tile_wqkv(Wq, Wk, Wv):
    out = np.zeros((NKV, NK, 128, 768), np.float32)
    for g in range(NKV):
        cat = np.concatenate([Wq[:, g * 512:(g + 1) * 512],
                              Wk[:, g * 128:(g + 1) * 128],
                              Wv[:, g * 128:(g + 1) * 128]], axis=1)
        out[g] = cat.reshape(NK, 128, 768)
    return out


def make_rope_tables(n_sw, pos0):
    inv = 1.0 / (10000.0 ** (2.0 * np.arange(64) / 128.0))
    ropek = np.zeros((n_sw, 2, 128, 128), np.float32)
    for s in range(n_sw):
        pos = pos0 + s * 128 + np.arange(128)
        ang = pos[:, None].astype(np.float64) * inv[None, :]
        c = np.cos(ang)
        si = np.sin(ang)
        ropek[s, 0, :, 0:64] = c
        ropek[s, 0, :, 64:128] = c
        ropek[s, 1, :, 0:64] = -si
        ropek[s, 1, :, 64:128] = si
    ropeq = (ropek * INV_SQRT_D).astype(np.float32)
    return ropeq, ropek


def make_masks(first):
    t = np.arange(128)[:, None]
    w = np.arange(192)[None, :]
    m = (w >= t) & (w <= t + 64)
    if first:
        m = m & (w >= 64)
    m = m.astype(np.float32)
    return np.tile(m, (1, 4))


def make_trimask():
    s = np.arange(128)[:, None]
    t = np.arange(128)[None, :]
    m = 0.5 * (s <= t).astype(np.float32)
    return np.tile(m, (1, 4))


_BUILD_CACHE = {}


def _get_program():
    key = "full"
    if key not in _BUILD_CACHE:
        _BUILD_CACHE[key] = build_program(
            phases=("p0", "ppre", "p1", "p2", "p3", "p4"))
    return _BUILD_CACHE[key]


def kernel(L0_attn_hidden_states, L0_attn_Wq, L0_attn_Wk, L0_attn_Wv,
           L0_attn_Wo, L0_ffn_W_base, L0_ffn_b_base, L0_ffn_W_router,
           L0_ffn_A, L0_ffn_B):
    x = np.asarray(L0_attn_hidden_states)
    Wq = np.asarray(L0_attn_Wq)
    Wk = np.asarray(L0_attn_Wk)
    Wv = np.asarray(L0_attn_Wv)
    Wo = np.asarray(L0_attn_Wo)
    Wb = np.asarray(L0_ffn_W_base)
    bb = np.asarray(L0_ffn_b_base)
    Wr = np.asarray(L0_ffn_W_router)
    A = np.asarray(L0_ffn_A)
    B = np.asarray(L0_ffn_B)

    wqkv = tile_wqkv(Wq, Wk, Wv)
    wo_t = np.ascontiguousarray(
        Wo.reshape(NK, 128, NK, 128).transpose(2, 0, 1, 3))
    wbT_t = np.ascontiguousarray(
        Wb.T.reshape(NK, 128, NK, 128).transpose(2, 0, 1, 3))
    wr_t = np.ascontiguousarray(Wr.T.reshape(NK, 128, 8))
    at_t = np.ascontiguousarray(
        A.reshape(128, D).T.reshape(NK, 128, 128))
    bf_t = np.ascontiguousarray(
        B.transpose(0, 2, 1).reshape(128, D).reshape(128, NK, 128)
        .transpose(1, 0, 2))
    b_t = np.ascontiguousarray(bb.reshape(NK, 128).T)
    consts = make_consts()
    maskR = make_masks(False)
    maskAf = make_masks(True)
    trim = make_trimask()

    nc = _get_program()
    in_maps = []
    for core in range(N_CORES):
        s, pp = core // 4, core % 4
        own0 = pp * OWN_TOK
        lo = own0 - N_PRE_CH * 128
        xc = np.zeros((TOK_ALL, D), np.float32)
        src0 = max(lo, 0)
        xc[src0 - lo:N_PRE_CH * 128 + OWN_TOK] = x[s, src0:own0 + OWN_TOK]
        ropeq, ropek = make_rope_tables(N_OWN_CH + 1, own0 - 128)
        in_maps.append({
            "xT_in": make_xT_tiled(xc),
            "consts": consts,
            "wqkv_in": wqkv,
            "ropeq_in": ropeq,
            "ropek_in": ropek,
            "maskR_in": maskR,
            "maskA_in": maskAf if pp == 0 else maskR,
            "trimask_in": trim,
            "wo_in": wo_t,
            "wbT_in": wbT_t,
            "wr_in": wr_t,
            "at_in": at_t,
            "bf_in": bf_t,
            "b_in": b_t,
        })

    res = run_bass_kernel_spmd(nc, in_maps, core_ids=list(range(N_CORES)))
    out = np.empty((2, 8192, D), np.float32)
    for core in range(N_CORES):
        s, pp = core // 4, core % 4
        out[s, pp * OWN_TOK:(pp + 1) * OWN_TOK] = res.results[core]["out_hbm"]
    return out


# version 06
# speedup vs baseline: 1.1619x; 1.1274x over previous
import numpy as np
import ml_dtypes
import concourse.bass as bass
import concourse.mybir as mybir
from concourse.bass_utils import run_bass_kernel_spmd
from contextlib import ExitStack

F32 = mybir.dt.float32
BF16 = mybir.dt.bfloat16
AF = mybir.ActivationFunctionType
ALU = mybir.AluOpType
AX = mybir.AxisListType
BF = ml_dtypes.bfloat16

D = 4096
NK = D // 128
NH, NKV, HD = 32, 8, 128
GRP = NH // NKV
EPS = 1e-5
GATE_NORM = 16.0
INV_SQRT_D = 1.0 / np.sqrt(128.0)

N_CORES = 8
OWN_TOK = 2048
N_OWN_CH = OWN_TOK // 128
N_PRE_CH = 48
TOK_ALL = (N_PRE_CH + N_OWN_CH) * 128
CB = 16

SAFE_CHAIN = {"tensor"}



ENGINES = ("sync", "gpsimd", "tensor", "vector", "scalar")


class Trk:
    __slots__ = ("ready", "users")

    def __init__(self):
        self.ready = {}
        self.users = {}


class Prog:
    def __init__(self):
        self.streams = {e: [] for e in ENGINES}
        self.ctr = {}
        self.water = {e: {} for e in ENGINES}

    def _bump(self, sem, k):
        self.ctr[sem] = self.ctr.get(sem, 0) + k
        return self.ctr[sem]

    def _wait(self, eng, sem, val):
        if val <= 0:
            return
        if self.water[eng].get(sem, 0) >= val:
            return
        self.water[eng][sem] = val
        self.streams[eng].append(
            lambda E, _s=sem, _v=val: E.wait_ge(E._sems[_s], _v))

    def barrier_from(self, eng, sems):
        for s in sems:
            self._wait(eng, s, self.ctr.get(s, 0))

    def full_barrier(self):
        for e in ENGINES:
            for s in list(self.ctr):
                self._wait(e, s, self.ctr[s])

    def step(self, eng, fns, sem=None, inc=1, reads=(), writes=(), rw=(),
             dma=False):
        if not isinstance(fns, (list, tuple)):
            fns = [fns]
        reads = list(reads) + list(rw)
        writes = list(writes) + list(rw)
        for r in reads:
            for s, (v, peng) in r.ready.items():
                if peng == eng and eng in SAFE_CHAIN and not dma:
                    continue
                self._wait(eng, s, v)
        for w in writes:
            for s, v in w.users.items():
                self._wait(eng, s, v)
            for s, (v, peng) in w.ready.items():
                if peng == eng and eng in SAFE_CHAIN and not dma:
                    continue
                self._wait(eng, s, v)
        for fn in fns[:-1]:
            self.streams[eng].append(fn)
        if sem is None:
            self.streams[eng].append(fns[-1])
            val = None
        else:
            val = self._bump(sem, inc)
            self.streams[eng].append(
                lambda E, _f=fns[-1], _s=sem, _k=inc: _f(E).then_inc(E._sems[_s], _k))
            for r in reads:
                r.users[sem] = val
            for w in writes:
                w.ready[sem] = (val, None if dma else eng)
                w.users = {}
        return val

    def dma(self, eng, out, in_, sem, reads=(), writes=(), **kw):
        return self.step(
            eng, lambda E: E.dma_start(out=out, in_=in_, **kw),
            sem=sem, inc=16, reads=reads, writes=writes, dma=True)


class Arena:

    def __init__(self, nc, name, nbytes):
        self.t = None
        self.nc = nc
        self.name = name
        self.nbytes = nbytes
        self.off = 0
        self.marks = []

    def open(self, ctx):
        self.t = ctx.enter_context(
            self.nc.sbuf_tensor(self.name, [128, self.nbytes], mybir.dt.uint8))
        return self

    def alloc(self, cols, dtype, ncopies=1):
        es = mybir.dt.size(dtype)
        out = []
        for _ in range(ncopies):
            off = (self.off + 63) & ~63
            self.off = off + cols * es
            assert self.off <= self.nbytes, (
                f"arena {self.name} overflow: {self.off} > {self.nbytes}")
            ap = self.t.ap()[:, off:off + cols * es].bitcast(dtype)
            out.append(ap)
        return out[0] if ncopies == 1 else out

    def push(self):
        self.marks.append(self.off)

    def pop(self):
        self.off = self.marks.pop()



def build_program(n_pre_ch=N_PRE_CH, n_own_ch=N_OWN_CH, phases=("p0",),
                  debug_taps=(), p1_level=99, p1_groups=NKV, o_parts="both"):
    n_ch = n_pre_ch + n_own_ch
    cb = min(CB, n_ch)
    assert n_ch % cb == 0
    n_batch = n_ch // cb

    nc = bass.Bass("TRN2", target_bir_lowering=False, debug=False)

    xT_in = nc.dram_tensor("xT_in", [n_ch, NK, 128, 128], F32,
                           kind="ExternalInput").ap()
    consts = nc.dram_tensor("consts", [8, 128, 128], F32,
                            kind="ExternalInput").ap()
    n_sw = n_own_ch + 1
    wqkv_in = ropeq_in = ropek_in = maskR_in = maskA_in = trimask_in = None
    if "ppre" in phases or "p1" in phases:
        wqkv_in = nc.dram_tensor("wqkv_in", [NKV, NK, 128, 768], F32,
                                 kind="ExternalInput").ap()
    if "p1" in phases:
        ropeq_in = nc.dram_tensor("ropeq_in", [n_sw, 2, 128, 128], F32,
                                  kind="ExternalInput").ap()
        ropek_in = nc.dram_tensor("ropek_in", [n_sw, 2, 128, 128], F32,
                                  kind="ExternalInput").ap()
        maskR_in = nc.dram_tensor("maskR_in", [128, 768], F32,
                                  kind="ExternalInput").ap()
        maskA_in = nc.dram_tensor("maskA_in", [128, 768], F32,
                                  kind="ExternalInput").ap()
        trimask_in = nc.dram_tensor("trimask_in", [128, 512], F32,
                                    kind="ExternalInput").ap()

    wo_in = wbT_in = wr_in = at_in = bf_in = b_in = None
    if "p2" in phases:
        wo_in = nc.dram_tensor("wo_in", [NK, NK, 128, 128], F32,
                               kind="ExternalInput").ap()
    if "p4" in phases:
        wbT_in = nc.dram_tensor("wbT_in", [NK, NK, 128, 128], F32,
                                kind="ExternalInput").ap()
        wr_in = nc.dram_tensor("wr_in", [NK, 128, 8], F32,
                               kind="ExternalInput").ap()
        at_in = nc.dram_tensor("at_in", [NK, 128, 128], F32,
                               kind="ExternalInput").ap()
        bf_in = nc.dram_tensor("bf_in", [NK, 128, 128], F32,
                               kind="ExternalInput").ap()
        b_in = nc.dram_tensor("b_in", [128, NK], F32,
                              kind="ExternalInput").ap()

    ln1T = nc.dram_tensor(
        "ln1T", [n_ch, NK, 128, 128], BF16,
        kind="ExternalOutput" if "ln1T" in debug_taps else "Internal").ap()
    oT_dram = None
    if "p1" in phases:
        oT_dram = nc.dram_tensor(
            "oT_dram", [n_own_ch, NKV, 128, 512], BF16,
            kind="ExternalOutput" if "oT" in debug_taps else "Internal").ap()
    dbg_bf = dbg_f32 = None
    if "gdbg" in debug_taps:
        dbg_bf = nc.dram_tensor("dbg_bf", [4, 128, 512], BF16,
                                kind="ExternalOutput").ap()
        dbg_f32 = nc.dram_tensor("dbg_f32", [2, 128, 512], F32,
                                 kind="ExternalOutput").ap()
    x1T_dram = None
    if "p2" in phases:
        x1T_dram = nc.dram_tensor(
            "x1T_dram", [NK, 128, n_own_ch * 128], BF16,
            kind="ExternalOutput" if "x1T" in debug_taps else "Internal").ap()
    out_hbm = None
    if "p4" in phases:
        out_hbm = nc.dram_tensor("out_hbm", [n_own_ch * 128, D], F32,
                                 kind="ExternalOutput").ap()
    ln2dbg = None
    if "ln2T" in debug_taps:
        ln2dbg = nc.dram_tensor("ln2dbg", [NK, 128, n_own_ch * 128], BF16,
                                kind="ExternalOutput").ap()
    sin_out = None
    if "Sin" in debug_taps:
        sin_out = nc.dram_tensor("sin_out", [NKV, 128, 128], F32,
                                 kind="ExternalOutput").ap()

    p = Prog()

    with ExitStack() as ctx:
        E = ctx.enter_context
        AR = Arena(nc, "arena", 205 * 1024).open(ctx)

        ident_f = AR.alloc(128, F32)
        tri16 = AR.alloc(128, F32)
        onehot = AR.alloc(1, F32)
        ones_col_f = AR.alloc(1, F32)
        ones_row_f = AR.alloc(128, F32)
        ones_col_b = AR.alloc(1, BF16)
        ident_b = AR.alloc(128, BF16)
        eps_col = AR.alloc(1, F32)

        t_cst = Trk()
        p.dma("sync", ident_f, consts[0], "CST", writes=[t_cst])
        p.dma("sync", tri16, consts[1], "CST", writes=[t_cst])
        p.dma("sync", onehot, consts[2, :, 0:1], "CST", writes=[t_cst],
              allow_slow_non_contiguous=True)
        p.dma("sync", ones_col_f, consts[3, :, 0:1], "CST", writes=[t_cst],
              allow_slow_non_contiguous=True)
        p.dma("sync", ones_row_f[0:1, :], consts[3, 0:1, :], "CST", writes=[t_cst])
        p.dma("gpsimd", ones_col_b, consts[3, :, 0:1], "CSG", writes=[t_cst],
              allow_slow_non_contiguous=True)
        p.dma("sync", eps_col, consts[4, :, 0:1], "CST", writes=[t_cst],
              allow_slow_non_contiguous=True)
        p.dma("gpsimd", ident_b, consts[0], "CSG", writes=[t_cst])

        AR.push()
        xbf = [AR.alloc(512, BF16) for _ in range(2)]
        x2b = [AR.alloc(512, BF16) for _ in range(2)]
        xf32 = [AR.alloc(cb * 128, F32) for _ in range(2)]
        lnout = [AR.alloc(cb * 128, BF16) for _ in range(2)]
        srow = AR.alloc(2 * cb * 128, F32)
        mrow = AR.alloc(cb * 128, F32)
        rrow = AR.alloc(cb * 128, F32)
        rep_m = AR.alloc(cb * 128, F32)
        rep_r = AR.alloc(cb * 128, F32)

        ps = [E(nc.psum_tensor(f"ps{i}", [128, 512], F32)) for i in range(8)]
        ps_sx, ps_sx2 = ps[0][0:1, :], ps[1][0:1, :]

        t_xbf = [Trk(), Trk()]
        t_x2b = [Trk(), Trk()]
        t_xf = [Trk(), Trk()]
        t_ln = [Trk(), Trk()]
        t_srow, t_mrow, t_rrow = Trk(), Trk(), Trk()
        t_rep = Trk()
        t_pssum = Trk()
        t_psrep = Trk()

        ntb = cb * 128 // 512
        for b in range(n_batch):
            ch0 = b * cb
            for tb in range(ntb):
                cA = ch0 + tb * 4
                for k in range(NK):
                    bi = k % 2
                    p.dma("gpsimd", xbf[bi].rearrange("p (c t) -> p c t", c=4),
                          xT_in[cA:cA + 4, k].rearrange("c p t -> p c t"),
                          f"XB{bi}", writes=[t_xbf[bi]])
                    p.step("scalar", lambda E, _i=bi: E.activation(
                        x2b[_i], xbf[_i], AF.Square), sem="AC",
                        reads=[t_xbf[bi]], writes=[t_x2b[bi]])
                    st_, sp_ = (k == 0), (k == NK - 1)
                    p.step("tensor", [
                        lambda E, _i=bi, _s=st_, _p=sp_: E.matmul(
                            ps_sx, ones_col_b, xbf[_i], start=_s, stop=_p),
                        lambda E, _i=bi, _s=st_, _p=sp_: E.matmul(
                            ps_sx2, ones_col_b, x2b[_i], start=_s, stop=_p),
                    ], sem="PE", reads=[t_xbf[bi], t_x2b[bi], t_cst],
                        writes=[t_pssum])
                p.step("scalar", [
                    lambda E, _t=tb: E.copy(srow[0:1, _t * 512:(_t + 1) * 512], ps_sx),
                    lambda E, _t=tb: E.copy(
                        srow[0:1, cb * 128 + _t * 512:cb * 128 + (_t + 1) * 512],
                        ps_sx2),
                ], sem="AC", rw=[t_pssum], writes=[t_srow])
            NT = cb * 128
            p.step("vector", lambda E: E.tensor_scalar(
                mrow[0:1, :], srow[0:1, 0:NT], 1.0 / D, None, op0=ALU.mult),
                sem="DV", reads=[t_srow], writes=[t_mrow])
            p.step("vector", lambda E: E.tensor_scalar(
                rrow[0:1, :], srow[0:1, NT:2 * NT], 1.0 / D, None, op0=ALU.mult),
                sem="DV", reads=[t_srow], writes=[t_rrow])
            p.step("scalar", lambda E: E.activation(
                srow[0:1, 0:NT], mrow[0:1, :], AF.Square), sem="AC",
                reads=[t_mrow], writes=[t_srow])
            p.step("vector", lambda E: E.tensor_tensor(
                rrow[0:1, :], rrow[0:1, :], srow[0:1, 0:NT], op=ALU.subtract),
                sem="DV", reads=[t_srow], rw=[t_rrow])
            p.step("scalar", lambda E: E.activation(
                rrow[0:1, :], rrow[0:1, :], AF.Ln, bias=eps_col[0:1, :]),
                sem="AC", rw=[t_rrow])
            p.step("scalar", lambda E: E.activation(
                rrow[0:1, :], rrow[0:1, :], AF.Exp, scale=-0.5),
                sem="AC", rw=[t_rrow])
            for piece in range(ntb):
                sl = slice(piece * 512, (piece + 1) * 512)
                p.step("tensor", [
                    lambda E, _s=sl: E.matmul(ps[2][:], ones_row_f[0:1, :],
                                              mrow[0:1, _s], start=True, stop=True),
                    lambda E, _s=sl: E.matmul(ps[3][:], ones_row_f[0:1, :],
                                              rrow[0:1, _s], start=True, stop=True),
                ], sem="PE", reads=[t_mrow, t_rrow, t_cst], writes=[t_psrep])
                p.step("scalar", [
                    lambda E, _s=sl: E.copy(rep_m[:, _s], ps[2][:]),
                    lambda E, _s=sl: E.copy(rep_r[:, _s], ps[3][:]),
                ], sem="AC", rw=[t_psrep], writes=[t_rep])
            for k in range(NK):
                bi = k % 2
                p.dma("sync", xf32[bi].rearrange("p (c t) -> p c t", c=cb),
                      xT_in[ch0:ch0 + cb, k].rearrange("c p t -> p c t"),
                      f"XF{bi}", writes=[t_xf[bi]])
                p.step("vector", lambda E, _i=bi: E.tensor_tensor(
                    xf32[_i], xf32[_i], rep_m, op=ALU.subtract),
                    sem="DV", reads=[t_rep], rw=[t_xf[bi]])
                p.step("vector", lambda E, _i=bi: E.tensor_tensor(
                    lnout[_i], xf32[_i], rep_r, op=ALU.mult),
                    sem="DV", reads=[t_rep, t_xf[bi]], writes=[t_ln[bi]])
                p.dma("sync",
                      ln1T[ch0:ch0 + cb, k].rearrange("c p t -> p c t"),
                      lnout[bi].rearrange("p (c t) -> p c t", c=cb),
                      f"LS{bi}", reads=[t_ln[bi]])
        AR.pop()

        if "ppre" in phases or "p1" in phases:
            p.full_barrier()
            AR.push()

            from types import SimpleNamespace

            S = [[AR.alloc(128, F32) for _ in range(2)] for _ in range(NKV)]
            S_bf = [[AR.alloc(128, BF16) for _ in range(2)] for _ in range(NKV)]
            t_S = [[Trk(), Trk()] for _ in range(NKV)]
            t_Sbf = [[Trk(), Trk()] for _ in range(NKV)]
            Vring = [AR.alloc(128, BF16) for _ in range(2)]
            t_V = [Trk(), Trk()]

            def mk_kv_ns():
                n = SimpleNamespace()
                for nm, cols, dt in [("ek", 128, F32), ("absk", 128, F32),
                                     ("gtmp", 128, F32), ("kmin", 128, F32),
                                     ("gs", 128, F32), ("em", 128, F32),
                                     ("ep", 128, F32), ("ktf", 128, F32),
                                     ("ktil", 128, BF16), ("sigk", 1, F32),
                                     ("recipk", 1, F32), ("gam", 1, F32),
                                     ("stmp", 128, F32), ("krb", 128, BF16),
                                     ("t1k", 128, F32), ("t2k", 128, F32),
                                     ("ktT", 128, BF16)]:
                    setattr(n, nm, AR.alloc(cols, dt))
                    setattr(n, "t_" + nm, Trk())
                return n

            KV = [mk_kv_ns(), mk_kv_ns()]
            FQ = [ps[0][:, :], ps[6][:, :]]
            t_FQ = [Trk(), Trk()]
            FKV = [ps[1][:, 0:256], ps[7][:, 0:256]]
            FG = [ps[1][:, 256:384], ps[7][:, 256:384]]
            FSp = [ps[1][:, 384:512], ps[7][:, 384:512]]
            t_FK = [Trk(), Trk()]
            ps_gam_p = [ps[1][:, 383:384], ps[7][:, 383:384]]
            t_ps2 = Trk()
            t_ps5 = Trk()
            t_sc34 = [Trk(), Trk()]
            tq_ctr = [0]

            def emit_kv_math(B, pr, do_gla, v_slot):
                if do_gla:
                    p.step("scalar", lambda E, _B=B, _P=FKV[pr]: E.activation(
                        _B.ek, _P[:, 0:128], AF.Exp, accum_out=_B.sigk),
                        sem="AC", rw=[t_FK[pr]], writes=[B.t_ek, B.t_sigk])
                    p.step("vector", lambda E, _B=B: E.reciprocal(
                        _B.recipk, _B.sigk), sem="DV",
                        reads=[B.t_sigk], writes=[B.t_recipk])
                    p.step("scalar", lambda E, _B=B, _P=FKV[pr]: E.activation(
                        _B.absk, _P[:, 0:128], AF.Abs), sem="AC",
                        rw=[t_FK[pr]], writes=[B.t_absk])
                    p.step("scalar", lambda E, _B=B: E.activation(
                        _B.gtmp, _B.absk, AF.Exp, scale=-1.0), sem="AC",
                        reads=[B.t_absk], writes=[B.t_gtmp])
                    p.step("scalar", lambda E, _B=B: E.activation(
                        _B.gtmp, _B.gtmp, AF.Ln, bias=ones_col_f), sem="AC",
                        rw=[B.t_gtmp])
                    p.step("vector", lambda E, _B=B, _P=FKV[pr]: E.tensor_scalar(
                        _B.kmin, _P[:, 0:128], 0.0, None, op0=ALU.min),
                        sem="DV", rw=[t_FK[pr]], writes=[B.t_kmin])
                    p.step("vector", lambda E, _B=B: E.tensor_tensor(
                        _B.gs, _B.kmin, _B.gtmp, op=ALU.subtract),
                        sem="DV", reads=[B.t_kmin, B.t_gtmp], writes=[B.t_gs])
                    p.step("tensor", lambda E, _B=B, _G=FG[pr]: E.matmul(
                        _G, tri16, _B.gs, start=True, stop=True),
                        sem="PE", reads=[B.t_gs, t_cst], rw=[t_FK[pr]])
                    p.step("scalar", [
                        lambda E, _B=B, _G=FG[pr]: E.activation(
                            _B.em, _G, AF.Exp, scale=-1.0),
                        lambda E, _B=B, _G=FG[pr]: E.activation(
                            _B.ep, _G, AF.Exp),
                    ], sem="AC", rw=[t_FK[pr]], writes=[B.t_em, B.t_ep])
                    p.step("vector", lambda E, _B=B: E.tensor_tensor(
                        _B.ktf, _B.ek, _B.em, op=ALU.mult),
                        sem="DV", reads=[B.t_ek, B.t_em], writes=[B.t_ktf])
                    p.step("vector", lambda E, _B=B: E.tensor_scalar(
                        _B.ktil, _B.ktf, _B.recipk, None, op0=ALU.mult),
                        sem="DV", reads=[B.t_ktf, B.t_recipk],
                        writes=[B.t_ktil])
                    p.step("tensor", lambda E, _B=B, _G=ps_gam_p[pr]:
                           E.matmul(_G, _B.ep, onehot, start=True, stop=True),
                           sem="PE", reads=[B.t_ep, t_cst], rw=[t_FK[pr]])
                    p.step("scalar", lambda E, _B=B, _G=ps_gam_p[pr]: E.copy(
                        _B.gam, _G), sem="AC", rw=[t_FK[pr]],
                        writes=[B.t_gam])
                p.step("scalar", lambda E, _s=v_slot, _P=FKV[pr]: E.copy(
                    Vring[_s], _P[:, 128:256]),
                    sem="AC", rw=[t_FK[pr]], writes=[t_V[v_slot]])

            def emit_state_update(B, pr, g, par, v_slot):
                p.step("tensor", lambda E, _B=B, _s=v_slot, _Sp=FSp[pr]:
                       E.matmul(_Sp, _B.ktil, Vring[_s], start=True,
                                stop=True),
                       sem="PE", reads=[B.t_ktil, t_V[v_slot]], rw=[t_FK[pr]])
                p.step("vector", lambda E, _B=B, _g=g, _p=par, _Sp=FSp[pr]:
                       E.tensor_tensor(_B.stmp, S[_g][1 - _p], _Sp,
                                       op=ALU.add),
                       sem="DV", reads=[t_S[g][1 - par]], rw=[t_FK[pr]],
                       writes=[B.t_stmp])
                p.step("vector", lambda E, _B=B, _g=g, _p=par: E.tensor_scalar(
                    S[_g][_p], _B.stmp, _B.gam, None, op0=ALU.mult),
                    sem="DV", reads=[B.t_stmp, B.t_gam], writes=[t_S[g][par]])

            def transpose_to(dst_ap, src_ap, t_src, t_dst, sem, qn="sync"):
                qn = "sync"
                p.step(qn, lambda E, _d=dst_ap, _s=src_ap: E.dma_start(
                    out=_d, in_=_s, transpose=True), sem=sem, inc=16,
                    reads=[t_src], writes=[t_dst], dma=True)

            def pe_transpose_to(dst_ap, src_ap, t_src, t_dst, slot_ap,
                                t_bank):
                p.step("tensor", lambda E, _a=slot_ap, _s=src_ap: E.transpose(
                    _a, _s, ident_b), sem="PE",
                    reads=[t_src, t_cst], rw=[t_bank])
                p.step("scalar", lambda E, _a=slot_ap, _d=dst_ap: E.copy(
                    _d, _a), sem="AC", rw=[t_bank], writes=[t_dst])

        if "ppre" in phases:
            AR.push()
            wkv = AR.alloc(NKV * NK * 256, BF16)
            wkv_v = wkv.rearrange("p (g k c) -> p g k c", g=NKV, k=NK)
            t_wkv = Trk()
            for g in range(NKV):
                p.dma("gpsimd", wkv_v[:, g],
                      wqkv_in[g, :, :, 512:768].rearrange("k p c -> p k c"),
                      "WKV", writes=[t_wkv])
            lnst = [AR.alloc(NK * 128, BF16) for _ in range(2)]
            t_lnst = [Trk(), Trk()]
            for g in range(NKV):
                p.step("vector", lambda E, _g=g: E.memset(S[_g][0], 0.0),
                       sem="DV", writes=[t_S[g][0]])
            it = 0
            for c in range(n_pre_ch):
                sl = c % 2
                p.dma("sync", lnst[sl].rearrange("p (k t) -> p k t", k=NK),
                      ln1T[c].rearrange("k p t -> p k t"), f"LN{sl}",
                      writes=[t_lnst[sl]])
                for g in range(NKV):
                    pr = it % 2
                    it += 1
                    par = (c + 1) % 2
                    B = KV[pr]
                    fns = []
                    for k in range(NK):
                        fns.append(lambda E, _g=g, _k=k, _s=sl, _P=FKV[pr]:
                                   E.matmul(
                            _P, lnst[_s].rearrange(
                                "p (k t) -> p k t", k=NK)[:, _k, :],
                            wkv_v[:, _g, _k, :],
                            start=(_k == 0), stop=(_k == NK - 1)))
                    p.step("tensor", fns, sem="PE",
                           reads=[t_lnst[sl], t_wkv], rw=[t_FK[pr]])
                    emit_kv_math(B, pr, do_gla=True, v_slot=0)
                    emit_state_update(B, pr, g, par, 0)
            if "Sin" in debug_taps:
                par = n_pre_ch % 2
                sdbg = AR.alloc(128, F32)
                t_sdbg = Trk()
                for g in range(NKV):
                    p.step("scalar", lambda E, _g=g, _p=par: E.copy(
                        sdbg, S[_g][_p]), sem="AC",
                        reads=[t_S[g][par]], writes=[t_sdbg])
                    p.dma("sync", sin_out[g], sdbg, "SO", reads=[t_sdbg])
            AR.pop()

        if "p1" in phases:
            p.full_barrier()
            AR.push()
            wqb = [AR.alloc(NK * 768, BF16) for _ in range(2)]
            t_wq = [Trk(), Trk()]
            tblq = AR.alloc(n_sw * 2 * 128, BF16)
            tblq_v = tblq.rearrange("p (s i c) -> p s i c", s=n_sw, i=2)
            tblk = AR.alloc(n_sw * 2 * 128, BF16)
            tblk_v = tblk.rearrange("p (s i c) -> p s i c", s=n_sw, i=2)
            maskR = AR.alloc(768, BF16)
            maskA = AR.alloc(768, BF16)
            trim = AR.alloc(512, BF16)
            t_tbl = Trk()
            p.dma("gpsimd", tblq_v, ropeq_in.rearrange("s i p c -> p s i c"),
                  "TBL", writes=[t_tbl])
            p.dma("gpsimd", tblk_v, ropek_in.rearrange("s i p c -> p s i c"),
                  "TB2", writes=[t_tbl])
            p.dma("gpsimd", maskR, maskR_in, "TB3", writes=[t_tbl])
            p.dma("gpsimd", maskA, maskA_in, "TB4", writes=[t_tbl])
            p.dma("gpsimd", trim, trimask_in, "TB5", writes=[t_tbl])

            lnp = [AR.alloc(NK * 128, BF16) for _ in range(3)]
            t_lnp = [Trk(), Trk(), Trk()]
            krT = [AR.alloc(128, BF16) for _ in range(2)]
            t_krT = [Trk(), Trk()]

            def mk_q_ns():
                n = SimpleNamespace()
                for nm, cols, dt in [("eq", 512, F32), ("sigq", 4, F32),
                                     ("recipq", 4, F32), ("qhu", 512, F32),
                                     ("qhat", 512, BF16), ("qrb", 512, BF16),
                                     ("qhT", 512, BF16), ("qrT", 512, BF16)]:
                    setattr(n, nm, AR.alloc(cols, dt))
                    setattr(n, "t_" + nm, Trk())
                return n

            QN = [mk_q_ns(), mk_q_ns()]
            qt1s = AR.alloc(512, F32)
            qt2s = AR.alloc(512, F32)
            t_qt1s, t_qt2s = Trk(), Trk()
            for _qn in QN:
                _qn.qt1, _qn.qt2 = qt1s, qt2s
                _qn.t_qt1, _qn.t_qt2 = t_qt1s, t_qt2s
            atm = AR.alloc(512, BF16)
            pu = AR.alloc(768, F32)
            pm = AR.alloc(768, F32)
            ssum4 = AR.alloc(4, F32)
            rswa = AR.alloc(4, F32)
            pb = AR.alloc(4 * 256, BF16)
            ptA = AR.alloc(512, BF16)
            ptB = AR.alloc(512, BF16)
            oT_sb = [AR.alloc(512, BF16) for _ in range(2)]
            t_atm, t_pu, t_pm, t_ssum, t_rswa, t_pb = (
                Trk(), Trk(), Trk(), Trk(), Trk(), Trk())
            t_ptA, t_ptB = Trk(), Trk()
            t_oT = [Trk(), Trk()]

            def sc_ap(h):
                bank = ps[3] if h < 2 else ps[4]
                off = (h % 2) * 256
                return bank[:, off:off + 256], t_sc34[0 if h < 2 else 1]

            FQ_b = [ps[0].ap().bitcast(BF16), ps[6].ap().bitcast(BF16)]
            FG_b = [ps[1].ap().bitcast(BF16), ps[7].ap().bitcast(BF16)]
            SC_b = [ps[3].ap().bitcast(BF16), ps[4].ap().bitcast(BF16)]

            p.step("vector", lambda E: E.memset(pb, 0.0), sem="DV",
                   writes=[t_pb])
            it = 0
            for g in range(p1_groups):
                wsl = g % 2
                p.dma("gpsimd", wqb[wsl].rearrange("p (k c) -> p k c", k=NK),
                      wqkv_in[g].rearrange("k p c -> p k c"), f"WQ{wsl}",
                      writes=[t_wq[wsl]])
                par0 = n_pre_ch % 2
                p.step("scalar", lambda E, _g=g, _p=par0: E.copy(
                    S_bf[_g][_p], S[_g][_p]), sem="AC",
                    reads=[t_S[g][par0]], writes=[t_Sbf[g][par0]])
                for c in range(n_pre_ch - 1, n_ch):
                    pr = it % 2
                    it += 1
                    sl = c % 2
                    lsl = c % 3
                    ti = c - (n_pre_ch - 1)
                    j = c - n_pre_ch
                    is_halo = (c == n_pre_ch - 1)
                    B = KV[pr]
                    Q = QN[pr]
                    wv_ = wqb[wsl].rearrange("p (k c) -> p k c", k=NK)
                    p.dma("sync", lnp[lsl].rearrange("p (k t) -> p k t", k=NK),
                          ln1T[c].rearrange("k p t -> p k t"), f"LP{lsl}",
                          writes=[t_lnp[lsl]])
                    fns = []
                    for k in range(NK):
                        fns.append(lambda E, _k=k, _s=lsl, _w=wv_, _P=FKV[pr]:
                                   E.matmul(
                            _P, lnp[_s].rearrange(
                                "p (k t) -> p k t", k=NK)[:, _k, :],
                            _w[:, _k, 512:768],
                            start=(_k == 0), stop=(_k == NK - 1)))
                    p.step("tensor", fns, sem="PE",
                           reads=[t_lnp[lsl], t_wq[wsl]], rw=[t_FK[pr]])
                    if not is_halo:
                        fns = []
                        for k in range(NK):
                            fns.append(lambda E, _k=k, _s=lsl, _w=wv_,
                                       _P=FQ[pr]: E.matmul(
                                _P, lnp[_s].rearrange(
                                    "p (k t) -> p k t", k=NK)[:, _k, :],
                                _w[:, _k, 0:512],
                                start=(_k == 0), stop=(_k == NK - 1)))
                        p.step("tensor", fns, sem="PE",
                               reads=[t_lnp[lsl], t_wq[wsl]], rw=[t_FQ[pr]])
                    emit_kv_math(B, pr, do_gla=(not is_halo), v_slot=sl)
                    if not is_halo:
                        pe_transpose_to(B.ktT, B.ktil, B.t_ktil, B.t_ktT,
                                        FG_b[pr][:, 512:640], t_FK[pr])
                    p.step("vector", lambda E, _B=B, _t=ti, _P=FKV[pr]:
                           E.tensor_tensor(
                        _B.t1k, _P[:, 0:128], tblk_v[:, _t, 0, :],
                        op=ALU.mult),
                        sem="DV", reads=[t_tbl], rw=[t_FK[pr]],
                        writes=[B.t_t1k])
                    p.step("vector", [
                        lambda E, _B=B, _t=ti, _P=FKV[pr]: E.tensor_tensor(
                            _B.t2k[:, 0:64], _P[:, 64:128],
                            tblk_v[:, _t, 1, 0:64], op=ALU.mult),
                        lambda E, _B=B, _t=ti, _P=FKV[pr]: E.tensor_tensor(
                            _B.t2k[:, 64:128], _P[:, 0:64],
                            tblk_v[:, _t, 1, 64:128], op=ALU.mult),
                    ], sem="DV", reads=[t_tbl], rw=[t_FK[pr]],
                        writes=[B.t_t2k])
                    p.step("vector", lambda E, _B=B: E.tensor_tensor(
                        _B.krb, _B.t1k, _B.t2k, op=ALU.add),
                        sem="DV", reads=[B.t_t1k, B.t_t2k], writes=[B.t_krb])
                    transpose_to(krT[sl], B.krb, B.t_krb, t_krT[sl],
                                 f"TR{sl}", ("sync", "scalar")[sl])
                    if is_halo:
                        continue
                    for h in range(GRP):
                        p.step("scalar", lambda E, _Q=Q, _h=h, _P=FQ[pr]:
                               E.activation(
                            _Q.eq[:, _h * 128:(_h + 1) * 128],
                            _P[:, _h * 128:(_h + 1) * 128], AF.Exp,
                            accum_out=_Q.sigq[:, _h:_h + 1]), sem="AC",
                            rw=[t_FQ[pr]], writes=[Q.t_eq, Q.t_sigq])
                    p.step("vector", lambda E, _Q=Q: E.reciprocal(
                        _Q.recipq, _Q.sigq), sem="DV",
                        reads=[Q.t_sigq], writes=[Q.t_recipq])
                    p.step("vector", lambda E, _Q=Q, _B=B: E.tensor_tensor(
                        _Q.qhu.rearrange("p (h c) -> p h c", h=GRP),
                        _Q.eq.rearrange("p (h c) -> p h c", h=GRP),
                        _B.ep.unsqueeze(1).broadcast_to([128, GRP, 128]),
                        op=ALU.mult),
                        sem="DV", reads=[Q.t_eq, B.t_ep], writes=[Q.t_qhu])
                    for h in range(GRP):
                        p.step("vector", lambda E, _Q=Q, _h=h: E.tensor_scalar(
                            _Q.qhat[:, _h * 128:(_h + 1) * 128],
                            _Q.qhu[:, _h * 128:(_h + 1) * 128],
                            _Q.recipq[:, _h:_h + 1], None, op0=ALU.mult),
                            sem="DV", reads=[Q.t_qhu, Q.t_recipq],
                            writes=[Q.t_qhat])
                    p.step("vector", lambda E, _Q=Q, _t=ti, _P=FQ[pr]:
                           E.tensor_tensor(
                        _Q.qt1.rearrange("p (h c) -> p h c", h=GRP),
                        _P.rearrange("p (h c) -> p h c", h=GRP),
                        tblq_v[:, _t, 0, :].unsqueeze(1).broadcast_to(
                            [128, GRP, 128]), op=ALU.mult),
                        sem="DV", reads=[t_tbl], rw=[t_FQ[pr]],
                        writes=[Q.t_qt1])
                    p.step("vector", [
                        lambda E, _Q=Q, _t=ti, _P=FQ[pr]: E.tensor_tensor(
                            _Q.qt2.rearrange("p (h a c) -> p h a c", h=GRP,
                                             a=2)[:, :, 0, :],
                            _P.rearrange("p (h a c) -> p h a c", h=GRP,
                                         a=2)[:, :, 1, :],
                            tblq_v[:, _t, 1, 0:64].unsqueeze(1).broadcast_to(
                                [128, GRP, 64]), op=ALU.mult),
                        lambda E, _Q=Q, _t=ti, _P=FQ[pr]: E.tensor_tensor(
                            _Q.qt2.rearrange("p (h a c) -> p h a c", h=GRP,
                                             a=2)[:, :, 1, :],
                            _P.rearrange("p (h a c) -> p h a c", h=GRP,
                                         a=2)[:, :, 0, :],
                            tblq_v[:, _t, 1, 64:128].unsqueeze(1).broadcast_to(
                                [128, GRP, 64]), op=ALU.mult),
                    ], sem="DV", reads=[t_tbl], rw=[t_FQ[pr]],
                        writes=[Q.t_qt2])
                    p.step("vector", lambda E, _Q=Q: E.tensor_tensor(
                        _Q.qrb, _Q.qt1, _Q.qt2, op=ALU.add),
                        sem="DV", reads=[Q.t_qt1, Q.t_qt2], writes=[Q.t_qrb])
                    for h in range(GRP):
                        pe_transpose_to(Q.qhT[:, h * 128:(h + 1) * 128],
                                        Q.qhat[:, h * 128:(h + 1) * 128],
                                        Q.t_qhat, Q.t_qhT,
                                        FQ_b[pr][:, h * 128:(h + 1) * 128],
                                        t_FQ[pr])
                        pe_transpose_to(Q.qrT[:, h * 128:(h + 1) * 128],
                                        Q.qrb[:, h * 128:(h + 1) * 128],
                                        Q.t_qrb, Q.t_qrT,
                                        FQ_b[pr][:, 512 + h * 128:
                                                  512 + (h + 1) * 128],
                                        t_FQ[pr])
                    p.step("tensor", lambda E, _B=B, _Q=Q: E.matmul(
                        ps[2][:, 0:512], _B.ktT, _Q.qhT, start=True,
                        stop=True), sem="PE",
                        reads=[B.t_ktT, Q.t_qhT], rw=[t_ps2])
                    p.step("vector", lambda E: E.tensor_tensor(
                        atm, ps[2][:, 0:512], trim, op=ALU.mult), sem="DV",
                        reads=[t_tbl], rw=[t_ps2], writes=[t_atm])
                    for h in range(GRP):
                        ap_sc, t_sc = sc_ap(h)
                        p.step("tensor", [
                            lambda E, _Q=Q, _h=h, _a=ap_sc, _s=sl: E.matmul(
                                _a[:, 64:128],
                                _Q.qrT[:, _h * 128:(_h + 1) * 128],
                                krT[1 - _s][:, 64:128], start=True,
                                stop=True),
                            lambda E, _Q=Q, _h=h, _a=ap_sc, _s=sl: E.matmul(
                                _a[:, 128:256],
                                _Q.qrT[:, _h * 128:(_h + 1) * 128],
                                krT[_s], start=True, stop=True),
                        ], sem="PE",
                            reads=[Q.t_qrT, t_krT[sl], t_krT[1 - sl]],
                            rw=[t_sc])
                    for h in range(GRP):
                        ap_sc, t_sc = sc_ap(h)
                        p.step("scalar", lambda E, _h=h, _a=ap_sc:
                               E.activation(
                            pu[:, _h * 192:(_h + 1) * 192], _a[:, 64:256],
                            AF.Exp), sem="AC", rw=[t_sc], writes=[t_pu])
                    mk = maskA if j == 0 else maskR
                    p.step("vector", lambda E, _m=mk: E.tensor_tensor(
                        pm, pu, _m, op=ALU.mult), sem="DV",
                        reads=[t_pu, t_tbl], writes=[t_pm])
                    p.step("vector", lambda E: E.tensor_reduce(
                        ssum4, pm.rearrange("p (h c) -> p h c", h=GRP),
                        axis=AX.X, op=ALU.add), sem="DV",
                        reads=[t_pm], writes=[t_ssum])
                    p.step("vector", lambda E: E.reciprocal(rswa, ssum4),
                           sem="DV", reads=[t_ssum], writes=[t_rswa])
                    for h in range(GRP):
                        p.step("vector", lambda E, _h=h: E.tensor_scalar(
                            pb[:, _h * 256 + 64:(_h + 1) * 256],
                            pm[:, _h * 192:(_h + 1) * 192],
                            rswa[:, _h:_h + 1], 0.5, op0=ALU.mult,
                            op1=ALU.mult),
                            sem="DV", reads=[t_pm, t_rswa], writes=[t_pb])
                    for h in range(GRP):
                        bsel = 0 if h < 2 else 1
                        off = (h % 2) * 512
                        pe_transpose_to(ptA[:, h * 128:(h + 1) * 128],
                                        pb[:, h * 256:h * 256 + 128],
                                        t_pb, t_ptA,
                                        SC_b[bsel][:, off:off + 128],
                                        t_sc34[bsel])
                        pe_transpose_to(ptB[:, h * 128:(h + 1) * 128],
                                        pb[:, h * 256 + 128:(h + 1) * 256],
                                        t_pb, t_ptB,
                                        SC_b[bsel][:, off + 128:off + 256],
                                        t_sc34[bsel])
                    par = c % 2
                    mmops = []
                    if o_parts in ("both", "gla"):
                        mmops.append((Vring[sl], atm, None, t_atm))
                        mmops.append((S_bf[g][par], Q.qhT, None, Q.t_qhT))
                    if o_parts in ("both", "swa"):
                        for h in range(GRP):
                            mmops.append((Vring[1 - sl][64:128, :],
                                          ptA[64:128,
                                              h * 128:(h + 1) * 128],
                                          (h * 128, (h + 1) * 128), t_ptA))
                            mmops.append((Vring[sl],
                                          ptB[:, h * 128:(h + 1) * 128],
                                          (h * 128, (h + 1) * 128), t_ptB))
                    ofns = []
                    for i2, (lh, rh, cols, _) in enumerate(mmops):
                        outap = (ps[5][:, :] if cols is None
                                 else ps[5][:, cols[0]:cols[1]])
                        ofns.append(lambda E, _l=lh, _r=rh, _o=outap,
                                    _s=(i2 == 0), _p=(i2 == len(mmops) - 1):
                                    E.matmul(_o, _l, _r, start=_s, stop=_p))
                    p.step("tensor", ofns, sem="PE",
                           reads=[t_V[sl], t_V[1 - sl], t_atm,
                                  t_Sbf[g][par], Q.t_qhT, t_ptA, t_ptB],
                           rw=[t_ps5])
                    p.step("scalar", lambda E, _s=sl: E.copy(
                        oT_sb[_s], ps[5][:, :]), sem="AC",
                        rw=[t_ps5], writes=[t_oT[sl]])
                    p.dma("sync", oT_dram[j, g], oT_sb[sl], f"OS{sl}",
                          reads=[t_oT[sl]])
                    emit_state_update(B, pr, g, (c + 1) % 2, sl)
                    p.step("scalar", lambda E, _g=g, _p=(c + 1) % 2: E.copy(
                        S_bf[_g][_p], S[_g][_p]), sem="AC",
                        reads=[t_S[g][(c + 1) % 2]],
                        writes=[t_Sbf[g][(c + 1) % 2]])
            AR.pop()

        if "ppre" in phases or "p1" in phases:
            AR.pop()

        ntok = n_own_ch * 128
        nts = ntok // 512

        if "p2" in phases:
            p.full_barrier()
            AR.push()
            oTv = AR.alloc(n_own_ch * NKV * 512, BF16).rearrange(
                "p (j g c) -> p j g c", j=n_own_ch, g=NKV)
            t_oTr = Trk()
            for j_ in range(n_own_ch):
                p.dma("gpsimd", oTv[:, j_], oT_dram[j_].rearrange(
                    "g p c -> p g c"), "OTL", writes=[t_oTr])
            wot = [AR.alloc(NK * 128, BF16) for _ in range(2)]
            t_wot = [Trk(), Trk()]
            xts = [AR.alloc(512, F32) for _ in range(2)]
            t_xts = [Trk(), Trk()]
            x1b = [AR.alloc(512, BF16) for _ in range(2)]
            t_x1b = [Trk(), Trk()]
            t_ps0 = Trk()
            for jc in range(NK):
                wsl = jc % 2
                p.dma("gpsimd", wot[wsl].rearrange("p (h c) -> p h c", h=NK),
                      wo_in[jc].rearrange("h p c -> p h c"), f"WO{wsl}",
                      writes=[t_wot[wsl]])
                for ts in range(nts):
                    xsl = ts % 2
                    j0 = ts * 4
                    p.dma("sync", xts[xsl].rearrange("p (c t) -> p c t", c=4),
                          xT_in[n_pre_ch + j0:n_pre_ch + j0 + 4, jc].rearrange(
                              "c p t -> p c t"), f"XS{xsl}",
                          writes=[t_xts[xsl]])
                    fns = []
                    for hd in range(NK):
                        gg, hh = hd // GRP, hd % GRP
                        fns.append(lambda E, _h=hd, _g=gg, _hh=hh, _j=j0,
                                   _w=wsl: E.matmul(
                            ps[0][:, :],
                            wot[_w].rearrange("p (h c) -> p h c", h=NK)[:, _h, :],
                            oTv[:, _j:_j + 4, _g,
                                _hh * 128:(_hh + 1) * 128],
                            start=(_h == 0), stop=(_h == NK - 1)))
                    p.step("tensor", fns, sem="PE",
                           reads=[t_oTr, t_wot[wsl]], writes=[t_ps0])
                    p.step("vector", lambda E, _x=xsl, _b=xsl: E.tensor_tensor(
                        x1b[_b], ps[0][:, :], xts[_x], op=ALU.add),
                        sem="DV", reads=[t_xts[xsl]], rw=[t_ps0],
                        writes=[t_x1b[xsl]])
                    p.dma("sync", x1T_dram[jc, :, ts * 512:(ts + 1) * 512],
                          x1b[xsl], f"XW{xsl}", reads=[t_x1b[xsl]])
            AR.pop()

        if "p3" in phases:
            p.full_barrier()
            AR.push()
            ln2T = AR.alloc(NK * ntok, BF16).rearrange(
                "p (k t) -> p k t", k=NK)
            t_ln2 = Trk()
            AR.push()
            x1s = [AR.alloc(ntok, BF16) for _ in range(2)]
            t_x1s = [Trk(), Trk()]
            x2s = AR.alloc(ntok, BF16)
            t_x2s = Trk()
            srow2 = AR.alloc(2 * ntok, F32)
            m2row = AR.alloc(ntok, F32)
            r2row = AR.alloc(ntok, F32)
            t_sr2, t_m2, t_r2 = Trk(), Trk(), Trk()
            rep2_m = AR.alloc(ntok, F32)
            rep2_r = AR.alloc(ntok, F32)
            t_rep2 = Trk()
            t_pst = [Trk() for _ in range(8)]
            for jc in range(NK):
                xsl = jc % 2
                p.dma("sync", x1s[xsl], x1T_dram[jc], f"X1{xsl}",
                      writes=[t_x1s[xsl]])
                p.step("scalar", lambda E, _x=xsl: E.activation(
                    x2s, x1s[_x], AF.Square), sem="AC",
                    reads=[t_x1s[xsl]], writes=[t_x2s])
                for ts in range(nts):
                    st_, sp_ = (jc == 0), (jc == NK - 1)
                    p.step("tensor", [
                        lambda E, _x=xsl, _t=ts, _s=st_, _p=sp_: E.matmul(
                            ps[_t][0:1, :], ones_col_b,
                            x1s[_x][:, _t * 512:(_t + 1) * 512],
                            start=_s, stop=_p),
                        lambda E, _t=ts, _s=st_, _p=sp_: E.matmul(
                            ps[4 + _t][0:1, :], ones_col_b,
                            x2s[:, _t * 512:(_t + 1) * 512],
                            start=_s, stop=_p),
                    ], sem="PE", reads=[t_x1s[xsl], t_x2s, t_cst],
                        writes=[t_pst[ts], t_pst[4 + ts]])
            for ts in range(nts):
                p.step("scalar", [
                    lambda E, _t=ts: E.copy(
                        srow2[0:1, _t * 512:(_t + 1) * 512], ps[_t][0:1, :]),
                    lambda E, _t=ts: E.copy(
                        srow2[0:1, ntok + _t * 512:ntok + (_t + 1) * 512],
                        ps[4 + _t][0:1, :]),
                ], sem="AC", rw=[t_pst[ts], t_pst[4 + ts]], writes=[t_sr2])
            p.step("vector", lambda E: E.tensor_scalar(
                m2row[0:1, :], srow2[0:1, 0:ntok], 1.0 / D, None, op0=ALU.mult),
                sem="DV", reads=[t_sr2], writes=[t_m2])
            p.step("vector", lambda E: E.tensor_scalar(
                r2row[0:1, :], srow2[0:1, ntok:2 * ntok], 1.0 / D, None,
                op0=ALU.mult), sem="DV", reads=[t_sr2], writes=[t_r2])
            p.step("scalar", lambda E: E.activation(
                srow2[0:1, 0:ntok], m2row[0:1, :], AF.Square), sem="AC",
                reads=[t_m2], writes=[t_sr2])
            p.step("vector", lambda E: E.tensor_tensor(
                r2row[0:1, :], r2row[0:1, :], srow2[0:1, 0:ntok],
                op=ALU.subtract), sem="DV", reads=[t_sr2], rw=[t_r2])
            p.step("scalar", lambda E: E.activation(
                r2row[0:1, :], r2row[0:1, :], AF.Ln, bias=eps_col[0:1, :]),
                sem="AC", rw=[t_r2])
            p.step("scalar", lambda E: E.activation(
                r2row[0:1, :], r2row[0:1, :], AF.Exp, scale=-0.5),
                sem="AC", rw=[t_r2])
            for piece in range(nts):
                sl2 = slice(piece * 512, (piece + 1) * 512)
                p.step("tensor", [
                    lambda E, _s=sl2: E.matmul(ps[0][:, :], ones_row_f[0:1, :],
                                               m2row[0:1, _s], start=True,
                                               stop=True),
                    lambda E, _s=sl2: E.matmul(ps[1][:, :], ones_row_f[0:1, :],
                                               r2row[0:1, _s], start=True,
                                               stop=True),
                ], sem="PE", reads=[t_m2, t_r2, t_cst],
                    writes=[t_pst[0], t_pst[1]])
                p.step("scalar", [
                    lambda E, _s=sl2: E.copy(rep2_m[:, _s], ps[0][:, :]),
                    lambda E, _s=sl2: E.copy(rep2_r[:, _s], ps[1][:, :]),
                ], sem="AC", rw=[t_pst[0], t_pst[1]], writes=[t_rep2])
            for jc in range(NK):
                xsl = jc % 2
                p.dma("sync", x1s[xsl], x1T_dram[jc], f"X1{xsl}",
                      writes=[t_x1s[xsl]])
                p.step("vector", lambda E, _x=xsl: E.tensor_tensor(
                    x1s[_x], x1s[_x], rep2_m, op=ALU.subtract),
                    sem="DV", reads=[t_rep2], rw=[t_x1s[xsl]])
                p.step("vector", lambda E, _x=xsl, _j=jc: E.tensor_tensor(
                    ln2T[:, _j, :], x1s[_x], rep2_r, op=ALU.mult),
                    sem="DV", reads=[t_rep2, t_x1s[xsl]], writes=[t_ln2])
                if ln2dbg is not None:
                    p.dma("sync", ln2dbg[jc], ln2T[:, jc, :], "L2D",
                          reads=[t_ln2])
            AR.pop()

        if "p4" in phases:
            p.full_barrier()
            wrt = AR.alloc(NK * 8, BF16).rearrange("p (k e) -> p k e", k=NK)
            att = AR.alloc(NK * 128, BF16).rearrange("p (k c) -> p k c", k=NK)
            bcol = AR.alloc(NK, F32)
            t_wrt, t_att, t_bcol = Trk(), Trk(), Trk()
            p.dma("gpsimd", wrt, wr_in.rearrange("k p e -> p k e"), "P4C",
                  writes=[t_wrt])
            p.dma("gpsimd", att, at_in.rearrange("k p c -> p k c"), "P4D",
                  writes=[t_att])
            p.dma("sync", bcol, b_in, "P4B", writes=[t_bcol])
            lgts = AR.alloc(8, F32)
            m1c = AR.alloc(1, F32)
            mk1 = AR.alloc(8, F32)
            tm8 = AR.alloc(8, F32)
            m2c = AR.alloc(1, F32)
            mk2 = AR.alloc(8, F32)
            w1c = AR.alloc(1, F32)
            w2c = AR.alloc(1, F32)
            wfld = AR.alloc(8, F32)
            hw = AR.alloc(128, BF16)
            hwT = AR.alloc(ntok, BF16)
            t_lg, t_m1c, t_mk1, t_tm8, t_m2c, t_mk2 = (
                Trk(), Trk(), Trk(), Trk(), Trk(), Trk())
            t_w1, t_w2, t_wf, t_hw, t_hwT = Trk(), Trk(), Trk(), Trk(), Trk()
            t_psr = Trk()
            t_psh = Trk()
            for tc in range(n_own_ch):
                tsl = slice(tc * 128, (tc + 1) * 128)
                fns = [lambda E, _k=k2, _t=tc: E.matmul(
                    ps[2][:, 0:8], ln2T[:, _k, _t * 128:(_t + 1) * 128],
                    wrt[:, _k, :], start=(_k == 0), stop=(_k == NK - 1))
                    for k2 in range(NK)]
                p.step("tensor", fns, sem="PE", reads=[t_ln2, t_wrt],
                       writes=[t_psr])
                p.step("vector", lambda E: E.tensor_copy(lgts, ps[2][:, 0:8]),
                       sem="DV", rw=[t_psr], writes=[t_lg])
                p.step("vector", lambda E: E.tensor_reduce(
                    m1c, lgts, axis=AX.X, op=ALU.max), sem="DV",
                    reads=[t_lg], writes=[t_m1c])
                p.step("vector", lambda E: E.tensor_tensor(
                    mk1, lgts, m1c.broadcast_to([128, 8]), op=ALU.is_ge),
                    sem="DV", reads=[t_lg, t_m1c], writes=[t_mk1])
                p.step("vector", lambda E: E.tensor_scalar(
                    tm8, mk1, -1e30, None, op0=ALU.mult), sem="DV",
                    reads=[t_mk1], writes=[t_tm8])
                p.step("vector", lambda E: E.tensor_tensor(
                    tm8, lgts, tm8, op=ALU.add), sem="DV",
                    reads=[t_lg], rw=[t_tm8])
                p.step("vector", lambda E: E.tensor_reduce(
                    m2c, tm8, axis=AX.X, op=ALU.max), sem="DV",
                    reads=[t_tm8], writes=[t_m2c])
                p.step("vector", lambda E: E.tensor_tensor(
                    mk2, tm8, m2c.broadcast_to([128, 8]), op=ALU.is_ge),
                    sem="DV", reads=[t_tm8, t_m2c], writes=[t_mk2])
                p.step("vector", lambda E: E.tensor_tensor(
                    w1c, m2c, m1c, op=ALU.subtract), sem="DV",
                    reads=[t_m1c, t_m2c], writes=[t_w1])
                p.step("scalar", lambda E: E.activation(
                    w1c, w1c, AF.Exp), sem="AC", rw=[t_w1])
                p.step("vector", lambda E: E.tensor_scalar(
                    w1c, w1c, 1.0, None, op0=ALU.add), sem="DV", rw=[t_w1])
                p.step("vector", lambda E: E.reciprocal(w1c, w1c),
                       sem="DV", rw=[t_w1])
                p.step("vector", lambda E: E.tensor_scalar(
                    w1c, w1c, 2.0, None, op0=ALU.mult), sem="DV", rw=[t_w1])
                p.step("vector", lambda E: E.tensor_scalar(
                    w2c, w1c, -1.0, 2.0, op0=ALU.mult, op1=ALU.add),
                    sem="DV", reads=[t_w1], writes=[t_w2])
                p.step("vector", lambda E: E.tensor_scalar(
                    wfld, mk1, w1c[:, 0:1], None, op0=ALU.mult), sem="DV",
                    reads=[t_mk1, t_w1], writes=[t_wf])
                p.step("vector", lambda E: E.tensor_scalar(
                    tm8, mk2, w2c[:, 0:1], None, op0=ALU.mult), sem="DV",
                    reads=[t_mk2, t_w2], rw=[t_tm8])
                p.step("vector", lambda E: E.tensor_tensor(
                    wfld, wfld, tm8, op=ALU.add), sem="DV",
                    reads=[t_tm8], rw=[t_wf])
                fns = [lambda E, _k=k2, _t=tc: E.matmul(
                    ps[3][:, 0:128], ln2T[:, _k, _t * 128:(_t + 1) * 128],
                    att[:, _k, :], start=(_k == 0), stop=(_k == NK - 1))
                    for k2 in range(NK)]
                p.step("tensor", fns, sem="PE", reads=[t_ln2, t_att],
                       writes=[t_psh])
                p.step("vector", lambda E: E.tensor_tensor(
                    hw.rearrange("p (e r) -> p e r", e=8),
                    ps[3][:, 0:128].rearrange("p (e r) -> p e r", e=8),
                    wfld.unsqueeze(2).broadcast_to([128, 8, 16]),
                    op=ALU.mult), sem="DV", reads=[t_wf], rw=[t_psh],
                    writes=[t_hw])
                p.step("scalar", lambda E, _t=tc: E.dma_start(
                    out=hwT[:, _t * 128:(_t + 1) * 128], in_=hw,
                    transpose=True), sem="THW", inc=16,
                    reads=[t_hw], writes=[t_hwT], dma=True)
            wbt = [AR.alloc(NK * 128, BF16) for _ in range(2)]
            bft = [AR.alloc(128, BF16) for _ in range(2)]
            t_wbt = [Trk(), Trk()]
            t_bft = [Trk(), Trk()]
            x1r = [AR.alloc(512, BF16) for _ in range(2)]
            t_x1r = [Trk(), Trk()]
            of32 = [AR.alloc(512, F32) for _ in range(2)]
            t_of = [Trk(), Trk()]
            og32 = [AR.alloc(512, F32) for _ in range(2)]
            t_og = [Trk(), Trk()]
            t_ps45 = Trk()
            t_ps67 = [Trk(), Trk()]
            for oc in range(NK):
                wsl = oc % 2
                p.dma("gpsimd", wbt[wsl].rearrange("p (k c) -> p k c", k=NK),
                      wbT_in[oc].rearrange("k p c -> p k c"), f"WB{wsl}",
                      writes=[t_wbt[wsl]])
                p.dma("gpsimd", bft[wsl], bf_in[oc], f"BF{wsl}",
                      writes=[t_bft[wsl]])
                for ts in range(nts):
                    xsl = ts % 2
                    fns = [lambda E, _k=k2, _t=ts, _w=wsl: E.matmul(
                        ps[4][:, :],
                        wbt[_w].rearrange("p (k c) -> p k c", k=NK)[:, _k, :],
                        ln2T[:, _k, _t * 512:(_t + 1) * 512],
                        start=(_k == 0), stop=False)
                        for k2 in range(NK)]
                    fns.append(lambda E, _t=ts, _w=wsl: E.matmul(
                        ps[4][:, :], bft[_w],
                        hwT[:, _t * 512:(_t + 1) * 512],
                        start=False, stop=True))
                    p.step("tensor", fns, sem="PE",
                           reads=[t_ln2, t_wbt[wsl], t_bft[wsl], t_hwT],
                           writes=[t_ps45])
                    p.dma("sync", x1r[xsl],
                          x1T_dram[oc, :, ts * 512:(ts + 1) * 512],
                          f"XR{xsl}", writes=[t_x1r[xsl]])
                    p.step("vector", lambda E, _o=oc, _x=xsl: E.tensor_scalar(
                        of32[_x], ps[4][:, :], bcol[:, _o:_o + 1], None,
                        op0=ALU.add), sem="DV", rw=[t_ps45],
                        reads=[t_bcol], writes=[t_of[xsl]])
                    p.step("vector", lambda E, _x=xsl: E.tensor_tensor(
                        og32[_x], of32[_x], x1r[_x], op=ALU.add),
                        sem="DV", reads=[t_of[xsl], t_x1r[xsl]],
                        writes=[t_og[xsl]])
                    for q4 in range(4):
                        pb_ = 6 + (q4 % 2)
                        p.step("tensor", lambda E, _q=q4, _x=xsl, _b=pb_:
                               E.transpose(ps[_b][:, 0:128],
                                           og32[_x][:, _q * 128:(_q + 1) * 128],
                                           ident_f), sem="PE",
                               reads=[t_og[xsl], t_cst],
                               rw=[t_ps67[q4 % 2]])
                        p.step("scalar", lambda E, _b=pb_, _x=xsl, _q=q4:
                               E.copy(of32[_x][:, _q * 128:(_q + 1) * 128],
                                      ps[_b][:, 0:128]), sem="AC",
                               rw=[t_ps67[q4 % 2]], writes=[t_of[xsl]])
                    p.dma("sync",
                          out_hbm[ts * 512:(ts + 1) * 512,
                                  oc * 128:(oc + 1) * 128].rearrange(
                              "(a t) f -> t a f", a=4),
                          of32[xsl].rearrange("p (a f) -> p a f", a=4),
                          f"OW{xsl}", reads=[t_of[xsl]])
            AR.pop()

        p.full_barrier()

        sems = {}
        for name in sorted(p.ctr):
            sems[name] = E(nc.semaphore(name))
        block = E(nc.Block())

        def run_stream(eng_name):
            def body(Eng):
                Eng._sems = sems
                for fn in p.streams[eng_name]:
                    fn(Eng)
            return body

        block.sync(run_stream("sync"))
        block.gpsimd(run_stream("gpsimd"))
        block.tensor(run_stream("tensor"))
        block.vector(run_stream("vector"))
        block.scalar(run_stream("scalar"))

    return nc



def make_consts():
    c = np.zeros((8, 128, 128), np.float32)
    c[0] = np.eye(128, dtype=np.float32)
    c[1] = (np.tril(np.ones((128, 128), np.float32)) / GATE_NORM).T
    c[2][127, 0] = 1.0
    c[3][:] = 1.0
    c[4][:] = EPS
    return c


def make_xT_tiled(x_core):
    tok = x_core.shape[0]
    n_ch = tok // 128
    t = x_core.reshape(n_ch, 128, NK, 128)
    return np.ascontiguousarray(t.transpose(0, 2, 3, 1))


def tile_wqkv(Wq, Wk, Wv):
    out = np.zeros((NKV, NK, 128, 768), np.float32)
    for g in range(NKV):
        cat = np.concatenate([Wq[:, g * 512:(g + 1) * 512],
                              Wk[:, g * 128:(g + 1) * 128],
                              Wv[:, g * 128:(g + 1) * 128]], axis=1)
        out[g] = cat.reshape(NK, 128, 768)
    return out


def make_rope_tables(n_sw, pos0):
    inv = 1.0 / (10000.0 ** (2.0 * np.arange(64) / 128.0))
    ropek = np.zeros((n_sw, 2, 128, 128), np.float32)
    for s in range(n_sw):
        pos = pos0 + s * 128 + np.arange(128)
        ang = pos[:, None].astype(np.float64) * inv[None, :]
        c = np.cos(ang)
        si = np.sin(ang)
        ropek[s, 0, :, 0:64] = c
        ropek[s, 0, :, 64:128] = c
        ropek[s, 1, :, 0:64] = -si
        ropek[s, 1, :, 64:128] = si
    ropeq = (ropek * INV_SQRT_D).astype(np.float32)
    return ropeq, ropek


def make_masks(first):
    t = np.arange(128)[:, None]
    w = np.arange(192)[None, :]
    m = (w >= t) & (w <= t + 64)
    if first:
        m = m & (w >= 64)
    m = m.astype(np.float32)
    return np.tile(m, (1, 4))


def make_trimask():
    s = np.arange(128)[:, None]
    t = np.arange(128)[None, :]
    m = 0.5 * (s <= t).astype(np.float32)
    return np.tile(m, (1, 4))


_BUILD_CACHE = {}


def _get_program():
    key = "full"
    if key not in _BUILD_CACHE:
        _BUILD_CACHE[key] = build_program(
            phases=("p0", "ppre", "p1", "p2", "p3", "p4"))
    return _BUILD_CACHE[key]


def kernel(L0_attn_hidden_states, L0_attn_Wq, L0_attn_Wk, L0_attn_Wv,
           L0_attn_Wo, L0_ffn_W_base, L0_ffn_b_base, L0_ffn_W_router,
           L0_ffn_A, L0_ffn_B):
    x = np.asarray(L0_attn_hidden_states)
    Wq = np.asarray(L0_attn_Wq)
    Wk = np.asarray(L0_attn_Wk)
    Wv = np.asarray(L0_attn_Wv)
    Wo = np.asarray(L0_attn_Wo)
    Wb = np.asarray(L0_ffn_W_base)
    bb = np.asarray(L0_ffn_b_base)
    Wr = np.asarray(L0_ffn_W_router)
    A = np.asarray(L0_ffn_A)
    B = np.asarray(L0_ffn_B)

    wqkv = tile_wqkv(Wq, Wk, Wv)
    wo_t = np.ascontiguousarray(
        Wo.reshape(NK, 128, NK, 128).transpose(2, 0, 1, 3))
    wbT_t = np.ascontiguousarray(
        Wb.T.reshape(NK, 128, NK, 128).transpose(2, 0, 1, 3))
    wr_t = np.ascontiguousarray(Wr.T.reshape(NK, 128, 8))
    at_t = np.ascontiguousarray(
        A.reshape(128, D).T.reshape(NK, 128, 128))
    bf_t = np.ascontiguousarray(
        B.transpose(0, 2, 1).reshape(128, D).reshape(128, NK, 128)
        .transpose(1, 0, 2))
    b_t = np.ascontiguousarray(bb.reshape(NK, 128).T)
    consts = make_consts()
    maskR = make_masks(False)
    maskAf = make_masks(True)
    trim = make_trimask()

    nc = _get_program()
    in_maps = []
    for core in range(N_CORES):
        s, pp = core // 4, core % 4
        own0 = pp * OWN_TOK
        lo = own0 - N_PRE_CH * 128
        xc = np.zeros((TOK_ALL, D), np.float32)
        src0 = max(lo, 0)
        xc[src0 - lo:N_PRE_CH * 128 + OWN_TOK] = x[s, src0:own0 + OWN_TOK]
        ropeq, ropek = make_rope_tables(N_OWN_CH + 1, own0 - 128)
        in_maps.append({
            "xT_in": make_xT_tiled(xc),
            "consts": consts,
            "wqkv_in": wqkv,
            "ropeq_in": ropeq,
            "ropek_in": ropek,
            "maskR_in": maskR,
            "maskA_in": maskAf if pp == 0 else maskR,
            "trimask_in": trim,
            "wo_in": wo_t,
            "wbT_in": wbT_t,
            "wr_in": wr_t,
            "at_in": at_t,
            "bf_in": bf_t,
            "b_in": b_t,
        })

    res = run_bass_kernel_spmd(nc, in_maps, core_ids=list(range(N_CORES)))
    out = np.empty((2, 8192, D), np.float32)
    for core in range(N_CORES):
        s, pp = core // 4, core % 4
        out[s, pp * OWN_TOK:(pp + 1) * OWN_TOK] = res.results[core]["out_hbm"]
    return out
